# version 18
# baseline (speedup 1.0000x reference)
import numpy as np
import ml_dtypes
import concourse.bass as bass
import concourse.mybir as mybir
from concourse.bass_utils import run_bass_kernel_spmd
from contextlib import ExitStack

F32 = mybir.dt.float32
BF16 = mybir.dt.bfloat16
AF = mybir.ActivationFunctionType
ALU = mybir.AluOpType
AXX = mybir.AxisListType.X

NCORES = 8
NTOK = 65536
TPC = NTOK // NCORES
DM = 1024
FF = 4096
NE = 8
H = 16
D = 32
SCALE = float(D) ** -0.5
LAMBDA_INIT = 0.8 - 0.6 * float(np.exp(-0.3))
EPS = 1e-5

TA = 256
TS = 512
NBLK = TPC // 128
NTILE = TPC // TA
NSUB = TPC // TS

ALL_Q = ("tensor", "vector", "scalar", "sync", "gpsimd")


class Sched:
    DMAQ = {"sync": "syncD", "gpsimd": "poolD"}

    def __init__(self):
        self.nodes = []
        self.eng_nodes = {e: [] for e in ALL_Q}
        self.last_write = {}
        self.readers = {}
        self.clock = {}
        self.dma_count = {"syncD": 0, "poolD": 0}
        self.cseq = {e: 0 for e in ALL_Q}
        self._bar = None
        self._bar_applied = set()

    EXCLUSIVE = frozenset(("pBig", "pA", "pB", "pC", "pS", "pBig0", "pBig1", "pBig2", "pBig3",
                           "pA0", "pA1", "pB0", "pB1"))

    def op(self, engine, fn, reads=(), writes=(), dma=False, force_sync=False):
        if any(r in self.EXCLUSIVE for r in reads):
            writes = tuple(writes) + tuple(r for r in reads if r in self.EXCLUSIVE)
            reads = tuple(r for r in reads if r not in self.EXCLUSIVE)
        if dma:
            q = self.DMAQ[engine]
            seq = self.dma_count[q]
            self.dma_count[q] += 1
        else:
            q = engine
            seq = self.cseq[engine]
            self.cseq[engine] += 1
        n = dict(engine=engine, fn=fn, dma=dma, queue=q, seq=seq,
                 marked=False, waits=[], clock=None)
        ecl = self.clock.setdefault(engine, {})

        def add_wait(m):
            mq, ms = m["queue"], m["seq"]
            if mq == engine and not m["dma"]:
                if engine == "tensor":
                    return
                if m["seq"] >= seq:
                    return
                if ecl.get(mq, -1) >= ms:
                    return
                m["marked"] = True
                n["waits"].append(m)
                ecl[mq] = ms
                return
            if dma and m["dma"] and m["engine"] == engine:
                return
            if ecl.get(mq, -1) >= ms:
                return
            if not m["dma"]:
                m["marked"] = True
            n["waits"].append(m)
            mc = m["clock"]
            for k, v in mc.items():
                if ecl.get(k, -1) < v:
                    ecl[k] = v
            if ecl.get(mq, -1) < ms:
                ecl[mq] = ms

        if self._bar is not None and engine not in self._bar_applied:
            self._bar_applied.add(engine)
            for m in self._bar:
                add_wait(m)

        for r in reads:
            m = self.last_write.get(r)
            if m is not None:
                add_wait(m)
        for r in writes:
            m = self.last_write.get(r)
            if m is not None:
                add_wait(m)
            for m in self.readers.get(r, {}).values():
                add_wait(m)
        for r in reads:
            self.readers.setdefault(r, {})[q] = n
        for r in writes:
            self.last_write[r] = n
            self.readers[r] = {}
        n["clock"] = dict(ecl)
        n["clock"][q] = seq
        self.nodes.append(n)
        self.eng_nodes[engine].append(n)
        return n

    def barrier(self):
        bar = []
        for e in ALL_Q:
            if self.eng_nodes[e]:
                bar.append(self.eng_nodes[e][-1])
        self._bar = bar
        self._bar_applied = set()

    def emit(self, block, sems):
        for e in ALL_Q:
            cnt = 0
            for n in self.eng_nodes[e]:
                if not n["dma"] and n["marked"]:
                    cnt += 1
                    n["inc_value"] = cnt

        def wait_val(m):
            if m["dma"]:
                return (m["queue"], 16 * (m["seq"] + 1))
            return (m["queue"], m["inc_value"])

        def make_section(ename):
            nodes = self.eng_nodes[ename]

            def section(eng):
                for n in nodes:
                    done = {}
                    for m in n["waits"]:
                        qn, v = wait_val(m)
                        if done.get(qn, -1) >= v:
                            continue
                        done[qn] = v
                        eng.wait_ge(sems[qn], v)
                    inst = n["fn"](eng)
                    if n["dma"]:
                        inst.then_inc(sems[n["queue"]], 16)
                    elif n["marked"]:
                        inst.then_inc(sems[ename], 1)
                if ename == "sync":
                    eng.wait_ge(sems["syncD"], 16 * self.dma_count["syncD"])
                    eng.wait_ge(sems["poolD"], 16 * self.dma_count["poolD"])
            return section

        for e in ALL_Q:
            if self.eng_nodes[e] or e == "sync":
                getattr(block, e)(make_section(e))


PHASES = "APRQCD"


def build_program():
    nc = bass.Bass("TRN2", target_bir_lowering=False, debug=False)

    din = lambda name, shape, dt=F32: nc.dram_tensor(name, list(shape), dt, kind="ExternalInput").ap()
    x_d = din("x", (TPC, DM))
    wqT = din("wqT", (DM, DM))
    wkT = din("wkT", (DM, DM))
    wvT = din("wvT", (DM, DM))
    woT = din("woT", (DM, DM))
    w1aT = din("w1aT", (DM, FF))
    w2aT = din("w2aT", (8, 128, 32 * 128))
    w1m = din("w1m", (NE, DM, FF))
    w2m = din("w2m", (NE, 8, 128, 32 * 128))
    centT_d = din("centT", (DM, NE))
    gb_d = din("gb", (128, 64))
    b1a_d = din("b1a", (128, 32))
    b1m_d = din("b1m", (128, NE * 32))
    b2m_d = din("b2m", (128, NE * 8))
    lam_d = din("lamv", (4, 32))
    csc = nc.dram_tensor("cscb", [128, 8 * 128], BF16, kind="ExternalInput").ap()
    cksel = nc.dram_tensor("cksel", [128, 4 * 128], BF16, kind="ExternalInput").ap()
    cbc = nc.dram_tensor("cbc", [16, 8 * 128], BF16, kind="ExternalInput").ap()
    crr = nc.dram_tensor("crr", [128, 8 * 128], BF16, kind="ExternalInput").ap()
    cesel = nc.dram_tensor("cesel", [8, NE * 128], BF16, kind="ExternalInput").ap()
    ceye = nc.dram_tensor("ceye", [128, 128], F32, kind="ExternalInput").ap()
    cg1r = nc.dram_tensor("cg1r", [1, DM], BF16, kind="ExternalInput").ap()
    cg2r = nc.dram_tensor("cg2r", [1, DM], BF16, kind="ExternalInput").ap()

    y_d = nc.dram_tensor("y", [TPC, DM], F32, kind="ExternalOutput").ap()

    import os as _os2
    _dk = "ExternalOutput" if _os2.environ.get("BISECT_DBGOUT") else "Internal"
    u_dram = nc.dram_tensor("u_dram", [DM, TPC], BF16, kind=_dk).ap()
    conn_dram = nc.dram_tensor("conn_dram", [DM, TPC], BF16, kind=_dk).ap()
    acc_dram = nc.dram_tensor("acc_dram", [DM, TPC], F32, kind=_dk).ap()
    gates_dram = nc.dram_tensor("gates_dram", [NE, TPC], BF16, kind=_dk).ap()
    s1_dram = nc.dram_tensor("s1_dram", [1, TPC], BF16, kind=_dk).ap()
    q_dram = nc.dram_tensor("q_dram", [DM, TPC], BF16, kind="Internal").ap()
    k_dram = nc.dram_tensor("k_dram", [DM, TPC], BF16, kind="Internal").ap()
    v_dram = nc.dram_tensor("v_dram", [DM, TPC], BF16, kind="Internal").ap()

    es = ExitStack()
    sb = lambda n_, s_, d_: es.enter_context(nc.sbuf_tensor(n_, s_, d_))
    psa = lambda n_, s_, d_: es.enter_context(nc.psum_tensor(n_, s_, d_))

    big0 = sb("big0", [128, 32768], BF16)
    import os as _os
    big1 = sb("big1", [128, 512 if _os.environ.get("BISECT_REALBUFS") else 32768], BF16)
    hbuf = sb("hbuf", [128, 16384], BF16)
    aren = sb("aren", [128, 15360], BF16)

    scT = sb("scT", [128, 8 * 128], BF16)
    kselT = sb("kselT", [128, 4 * 128], BF16)
    bcT = sb("bcT", [16, 8 * 128], BF16)
    rrT = sb("rrT", [128, 8 * 128], BF16)
    eselT = sb("eselT", [8, NE * 128], BF16)
    eyeT = sb("eyeT", [128, 128], F32)
    g1row = sb("g1row", [1, DM], BF16)
    g2row = sb("g2row", [1, DM], BF16)
    gbT = sb("gbT", [128, 64], F32)
    b1aT = sb("b1aT", [128, 32], F32)
    b1mT = sb("b1mT", [128, NE * 32], F32)
    b2mT = sb("b2mT", [128, NE * 8], F32)
    centTb = sb("centTb", [128, 8 * 8], F32)
    lq1t = sb("lq1t", [1, 32], F32)
    lq2t = sb("lq2t", [1, 32], F32)
    lk1t = sb("lk1t", [1, 32], F32)
    lk2t = sb("lk2t", [1, 32], F32)
    lsc = sb("lsc", [1, 8], F32)
    lscb = sb("lscb", [1, 2], BF16)
    centHi = sb("centHi", [128, 64], BF16)
    centLo = sb("centLo", [128, 64], BF16)
    ones1b = sb("ones1b", [1, 128], BF16)
    ones128 = sb("ones128", [128, 128], BF16)
    epsc = sb("epsc", [128, 1], F32)
    lcol = sb("lcol", [48, 1], F32)
    lnl16 = sb("lnl16", [16, 1], F32)
    mx16 = sb("mx16", [16, 1], F32)
    negc = sb("negc", [64, 1], F32)

    pBig = psa("pBig", [128, 2048], F32)
    pA = psa("pA", [128, 512], F32)
    pB = psa("pB", [128, 512], F32)
    pC = psa("pC", [128, 512], F32)
    pS = psa("pS", [128, 512], F32)

    sems = {}
    for s in ALL_Q + ("syncD", "poolD"):
        sems[s] = es.enter_context(nc.semaphore("sem_" + s))

    S = Sched()

    def pbk(lo, hi):
        return tuple(f"pBig{b}" for b in range(lo // 512, (hi + 511) // 512))

    def dma(engine, out_ap, in_ap, reads, writes, **kw):
        return S.op(engine, lambda e: e.dma_start(out=out_ap, in_=in_ap, **kw),
                    reads=reads, writes=writes, dma=True)

    def b0(off, n, dt=BF16):
        if dt == BF16:
            return big0[:, off:off + n]
        return big0[:, off:off + 2 * n].bitcast(F32)

    def ar(off, n, dt=BF16):
        if dt == BF16:
            return aren[:, off:off + n]
        return aren[:, off:off + 2 * n].bitcast(F32)

    dma("sync", scT[:], csc, (), ("scT",))
    dma("sync", kselT[:], cksel, (), ("kselT",))
    dma("sync", bcT[:], cbc, (), ("bcT",))
    dma("sync", rrT[:], crr, (), ("rrT",))
    dma("sync", eselT[:], cesel, (), ("eselT",))
    dma("sync", eyeT[:], ceye, (), ("eyeT",))
    dma("sync", g1row[:], cg1r, (), ("g1row",))
    dma("sync", g2row[:], cg2r, (), ("g2row",))
    dma("sync", gbT[:], gb_d, (), ("gbT",))
    dma("sync", b1aT[:], b1a_d, (), ("b1aT",))
    dma("sync", b1mT[:], b1m_d, (), ("b1mT",))
    dma("sync", b2mT[:], b2m_d, (), ("b2mT",))
    dma("sync", centTb[:].rearrange("p (c n) -> p c n", c=8),
        centT_d.rearrange("(c p) n -> p c n", p=128), (), ("centTb",))
    dma("sync", lq1t[:], lam_d[0:1, :], (), ("lq1t",))
    dma("sync", lq2t[:], lam_d[1:2, :], (), ("lq2t",))
    dma("sync", lk1t[:], lam_d[2:3, :], (), ("lk1t",))
    dma("sync", lk2t[:], lam_d[3:4, :], (), ("lk2t",))

    S.op("vector", lambda e: e.memset(ones1b[:], 1.0), (), ("ones1b",))
    S.op("vector", lambda e: e.memset(ones128[:], 1.0), (), ("ones128",))
    S.op("vector", lambda e: e.memset(epsc[:], EPS), (), ("epsc",))
    S.op("vector", lambda e: e.memset(lcol[:], 1.0), (), ("lcol",))
    S.op("vector", lambda e: e.memset(negc[:], -8.0), (), ("negc",))
    S.op("vector", lambda e: e.memset(lnl16[:], float(np.log(1.0 - LAMBDA_INIT))), (), ("lnl16",))

    S.op("vector", lambda e: e.tensor_tensor(lq1t[:], lq1t[:], lk1t[:], ALU.mult), ("lq1t", "lk1t"), ("lq1t",))
    S.op("vector", lambda e: e.tensor_tensor(lq2t[:], lq2t[:], lk2t[:], ALU.mult), ("lq2t", "lk2t"), ("lq2t",))
    S.op("vector", lambda e: e.reduce_sum(lsc[:, 0:1], lq1t[:], axis=AXX), ("lq1t",), ("lsc",))
    S.op("vector", lambda e: e.reduce_sum(lsc[:, 1:2], lq2t[:], axis=AXX), ("lq2t",), ("lsc",))
    S.op("scalar", lambda e: e.activation(lsc[:, 2:3], lsc[:, 0:1], AF.Exp), ("lsc",), ("lsc",))
    S.op("scalar", lambda e: e.activation(lsc[:, 3:4], lsc[:, 1:2], AF.Exp), ("lsc",), ("lsc",))
    S.op("vector", lambda e: e.tensor_tensor(lsc[:, 4:5], lsc[:, 2:3], lsc[:, 3:4], ALU.subtract), ("lsc",), ("lsc",))
    S.op("vector", lambda e: e.tensor_scalar(lsc[:, 4:5], lsc[:, 4:5], float(LAMBDA_INIT), None, ALU.add), ("lsc",), ("lsc",))
    S.op("vector", lambda e: e.tensor_scalar(lsc[:, 5:6], lsc[:, 4:5], -1.0, None, ALU.mult), ("lsc",), ("lsc",))
    S.op("vector", lambda e: e.tensor_copy(lscb[:, 0:1], lsc[:, 5:6]), ("lsc",), ("lscb",))
    S.op("vector", lambda e: e.tensor_tensor(lsc[:, 6:7], lsc[:, 5:6], lscb[:, 0:1], ALU.subtract), ("lsc", "lscb"), ("lsc",))
    S.op("vector", lambda e: e.tensor_copy(lscb[:, 1:2], lsc[:, 6:7]), ("lsc",), ("lscb",))
    S.op("tensor", lambda e: e.matmul(pS[:, 0:1], ones1b[:], lscb[:, 0:1], start=True, stop=False),
         ("ones1b", "lscb"), ("pS",))
    S.op("tensor", lambda e: e.matmul(pS[:, 0:1], ones1b[:], lscb[:, 1:2], start=False, stop=True),
         ("ones1b", "lscb"), ("pS",))
    S.op("vector", lambda e: e.tensor_copy(lcol[32:48, :], pS[32:48, 0:1]), ("pS",), ("lcol",))
    S.op("vector", lambda e: e.tensor_copy(centHi[:], centTb[:]), ("centTb",), ("centHi",))
    S.op("vector", lambda e: e.tensor_tensor(centLo[:], centTb[:], centHi[:], ALU.subtract), ("centTb", "centHi"), ("centLo",))

    import os
    phases = os.environ.get("BISECT_PHASES", PHASES)
    if os.environ.get("BISECT_REALBUFS"):
        xtmA = sb("xtmA_r", [128, 1024], F32)[:]
        xtmB = sb("xtmB_r", [128, 1024], F32)[:]
        xnf = sb("xnf_r", [128, 1024], F32)[:]
        ublk = sb("ublk_r", [128, 1024], BF16)[:]
        bn12 = sb("bn12_r", [128, 12], F32)[:]
        mvv = sb("mv_r", [128, 2], F32)[:]
        sAx = sb("sA_r", [128, 8], F32)[:]
        gtm = sb("gtm_r", [128, 8], F32)[:]
        gsc = sb("gsc_r", [128, 48], F32)[:]
        gstage = sb("gst_r", [128, 2048], BF16)[:]
        s1stage = sb("s1st_r", [128, 2048], BF16)[:]
        g0v = sb("g0_r", [128, 128], F32)[:]
        ulo = sb("ulo_r", [128, 1024], BF16)[:]
        AB = [dict(xnf=xnf, ublk=ublk, ulo=ulo, bn12=bn12, mvv=mvv, sAx=sAx, gtm=gtm, gsc=gsc)] * 2
    else:
        xtmA, xtmB = b0(0, 1024, F32), b0(2048, 1024, F32)
        gstage = b0(7616, 2048, BF16)
        s1stage = b0(9664, 2048, BF16)
        AB = []
        for par in range(2):
            base = 13056 + par * 4096
            AB.append(dict(
                xnf=b0(4096 + par * 1024 * 0, 1024, F32) if False else b0(base, 1024, F32),
                ublk=b0(base + 2048, 1024, BF16),
                ulo=b0(base + 3072, 1024, BF16),
                bn12=b0(11712 + par * 512, 12, F32),
                mvv=b0(11744 + par * 512, 2, F32),
                sAx=b0(11752 + par * 512, 8, F32),
                gtm=b0(11776 + par * 512, 8, F32),
                gsc=b0(11800 + par * 512, 48, F32),
            ))

    _nblkA = int(os.environ.get('BISECT_NBLK', NBLK))
    _aops = int(os.environ.get('BISECT_AOPS', 99))
    _actr = [0]
    def astep():
        _actr[0] += 1
        return _actr[0] <= _aops
    def emit_A_block(blk, P, sfx):
        xnf = P["xnf"]; ublk = P["ublk"]; ulo = P["ulo"]; bn12 = P["bn12"]
        mvv = P["mvv"]; sAx = P["sAx"]; gtm = P["gtm"]; gsc = P["gsc"]
        par = blk % 2
        boff = par * 1024
        psG, rpsG = (pS, "pS") if par == 0 else (pA, "pA")
        xt, xr = (xtmA, "xtmA") if blk % 2 == 0 else (xtmB, "xtmB")
        t0 = blk * 128
        if not astep(): return
        dma("sync", xt, x_d[t0:t0 + 128, :], (), (xr,))
        if not astep(): return
        S.op("vector", lambda e, xt=xt: e.bn_stats(bn12[:, 0:6], xt[:, 0:512]), (xr,), ("bn12" + sfx,))
        S.op("vector", lambda e, xt=xt: e.bn_stats(bn12[:, 6:12], xt[:, 512:1024]), (xr,), ("bn12" + sfx,))
        S.op("vector", lambda e: e.bn_aggr(mvv[:], bn12[:]), ("bn12" + sfx,), ("mv" + sfx,), force_sync=True)
        if not astep(): return
        S.op("scalar", lambda e: e.activation(sAx[:, 0:1], mvv[:, 1:2], AF.Ln, bias=epsc[:]), ("mv" + sfx, "epsc"), ("sA" + sfx,))
        S.op("scalar", lambda e: e.activation(sAx[:, 1:2], sAx[:, 0:1], AF.Exp, scale=-0.5), ("sA" + sfx,), ("sA" + sfx,))
        S.op("scalar", lambda e: e.activation(sAx[:, 2:3], sAx[:, 0:1], AF.Exp, scale=-1.0), ("sA" + sfx,), ("sA" + sfx,))
        S.op("vector", lambda e: e.tensor_tensor(sAx[:, 3:4], mvv[:, 1:2], sAx[:, 2:3], ALU.mult), ("sA" + sfx, "mv" + sfx), ("sA" + sfx,))
        S.op("scalar", lambda e: e.activation(sAx[:, 4:5], sAx[:, 3:4], AF.Ln, bias=epsc[:]), ("sA" + sfx, "epsc"), ("sA" + sfx,))
        S.op("scalar", lambda e: e.activation(sAx[:, 5:6], sAx[:, 4:5], AF.Exp, scale=-0.5), ("sA" + sfx,), ("sA" + sfx,))
        if not astep(): return
        S.op("vector", lambda e, xt=xt: e.tensor_scalar(xnf, xt, mvv[:, 0:1], sAx[:, 1:2], ALU.subtract, ALU.mult),
             (xr, "mv" + sfx, "sA" + sfx), ("xnf" + sfx,))
        if not astep(): return
        for c in range(8):
            S.op("tensor", lambda e, c=c, boff=boff: e.transpose(pBig[:, boff + c * 128:boff + (c + 1) * 128],
                                                                   xnf[:, c * 128:(c + 1) * 128], eyeT[:]),
                 ("xnf" + sfx, "eyeT"), pbk(boff + c * 128, boff + (c + 1) * 128))
        if not astep(): return
        S.op("scalar", lambda e, boff=boff: e.activation(ublk, pBig[:, boff:boff + 1024], AF.Copy), pbk(boff, boff + 1024), ("ublk" + sfx,))
        dma("sync", u_dram[:, t0:t0 + 128].rearrange("(c p) t -> p c t", p=128),
            ublk.rearrange("p (c t) -> p c t", c=8), ("ublk" + sfx,), ("u_dram",))
        if not astep(): return
        S.op("vector", lambda e, boff=boff: e.tensor_copy(xnf, pBig[:, boff:boff + 1024]), pbk(boff, boff + 1024), ("xnf" + sfx,))
        if not astep(): return
        S.op("vector", lambda e: e.tensor_tensor(ulo, xnf, ublk, ALU.subtract), ("xnf" + sfx, "ublk" + sfx), ("ulo" + sfx,))
        if not astep(): return
        for c in range(8):
            for ti, (act, ract, ce, rce) in enumerate(((ublk, "ublk" + sfx, centHi, "centHi"),
                                                       (ublk, "ublk" + sfx, centLo, "centLo"),
                                                       (ulo, "ulo" + sfx, centHi, "centHi"))):
                S.op("tensor", lambda e, c=c, act=act, ce=ce, ti=ti, psG=psG: e.matmul(
                    psG[:, 0:8], act[:, c * 128:(c + 1) * 128], ce[:, c * 8:(c + 1) * 8],
                    start=(c == 0 and ti == 0), stop=(c == 7 and ti == 2)),
                     (ract, rce), (rpsG,))
        if not astep(): return
        S.op("vector", lambda e, psG=psG: e.tensor_scalar(gsc[:, 8:16], psG[:, 0:8], -1.0, None, ALU.mult), (rpsG,), ("gsc" + sfx,))
        S.op("vector", lambda e, psG=psG: e.tensor_tensor(gsc[:, 40:48], psG[:, 0:8], gsc[:, 8:16], ALU.max), (rpsG, "gsc" + sfx), ("gsc" + sfx,))
        S.op("vector", lambda e, psG=psG: e.tensor_scalar(gsc[:, 16:24], psG[:, 0:8], 0.0, None, ALU.is_ge), (rpsG, "gsc" + sfx), ("gsc" + sfx,))
        S.op("scalar", lambda e: e.activation(gtm[:, 0:8], gsc[:, 40:48], AF.Exp, scale=-1.0), ("gsc" + sfx,), ("gtm" + sfx,))
        S.op("vector", lambda e: e.tensor_scalar(gsc[:, 32:40], gtm[:, 0:8], 1.0, None, ALU.add), ("gtm" + sfx, "gsc" + sfx), ("gsc" + sfx,))
        S.op("vector", lambda e: e.reciprocal(gsc[:, 32:40], gsc[:, 32:40]), ("gsc" + sfx,), ("gsc" + sfx,))
        S.op("vector", lambda e: e.tensor_tensor(gsc[:, 24:32], gtm[:, 0:8], gsc[:, 32:40], ALU.mult), ("gtm" + sfx, "gsc" + sfx), ("gsc" + sfx,))
        S.op("vector", lambda e: e.tensor_tensor(gsc[:, 8:16], gsc[:, 32:40], gsc[:, 24:32], ALU.subtract), ("gsc" + sfx,), ("gsc" + sfx,))
        S.op("vector", lambda e: e.tensor_tensor(gsc[:, 8:16], gsc[:, 16:24], gsc[:, 8:16], ALU.mult), ("gsc" + sfx,), ("gsc" + sfx,))
        S.op("vector", lambda e: e.tensor_tensor(gtm[:, 0:8], gsc[:, 24:32], gsc[:, 8:16], ALU.add), ("gsc" + sfx, "gtm" + sfx), ("gtm" + sfx,))
        if not astep(): return
        if not astep(): return
        S.op("vector", lambda e: e.reduce_max(sAx[:, 6:7], gtm[:, 0:8], axis=AXX), ("gtm" + sfx,), ("sA" + sfx,))
        S.op("vector", lambda e: e.tensor_scalar(gsc[:, 0:8], gtm[:, 0:8], sAx[:, 6:7], None, ALU.is_ge), ("gtm" + sfx, "sA" + sfx), ("gsc" + sfx,))

        def prefix8():
            S.op("vector", lambda e: e.tensor_copy(gsc[:, 8:9], gsc[:, 0:1]), ("gsc" + sfx,), ("gsc" + sfx,))
            S.op("vector", lambda e: e.tensor_tensor(gsc[:, 9:16], gsc[:, 1:8], gsc[:, 0:7], ALU.add), ("gsc" + sfx,), ("gsc" + sfx,))
            S.op("vector", lambda e: e.tensor_copy(gsc[:, 16:18], gsc[:, 8:10]), ("gsc" + sfx,), ("gsc" + sfx,))
            S.op("vector", lambda e: e.tensor_tensor(gsc[:, 18:24], gsc[:, 10:16], gsc[:, 8:14], ALU.add), ("gsc" + sfx,), ("gsc" + sfx,))
            S.op("vector", lambda e: e.tensor_copy(gsc[:, 24:28], gsc[:, 16:20]), ("gsc" + sfx,), ("gsc" + sfx,))
            S.op("vector", lambda e: e.tensor_tensor(gsc[:, 28:32], gsc[:, 20:24], gsc[:, 16:20], ALU.add), ("gsc" + sfx,), ("gsc" + sfx,))
            S.op("vector", lambda e: e.tensor_scalar(gsc[:, 24:32], gsc[:, 24:32], 1.0, None, ALU.is_equal), ("gsc" + sfx,), ("gsc" + sfx,))

        prefix8()
        S.op("vector", lambda e: e.tensor_tensor(gsc[:, 32:40], gsc[:, 24:32], gsc[:, 0:8], ALU.mult), ("gsc" + sfx,), ("gsc" + sfx,))
        S.op("vector", lambda e: e.scalar_tensor_tensor(gsc[:, 40:48], gsc[:, 32:40], -1e30, gtm[:, 0:8], ALU.mult, ALU.add),
             ("gsc" + sfx, "gtm" + sfx), ("gsc" + sfx,))
        S.op("vector", lambda e: e.reduce_max(sAx[:, 7:8], gsc[:, 40:48], axis=AXX), ("gsc" + sfx,), ("sA" + sfx,))
        S.op("vector", lambda e: e.tensor_scalar(gsc[:, 0:8], gsc[:, 40:48], sAx[:, 7:8], None, ALU.is_ge), ("gsc" + sfx, "sA" + sfx), ("gsc" + sfx,))
        prefix8()
        S.op("vector", lambda e: e.tensor_tensor(gsc[:, 0:8], gsc[:, 24:32], gsc[:, 0:8], ALU.mult), ("gsc" + sfx,), ("gsc" + sfx,))
        S.op("vector", lambda e: e.tensor_tensor(gsc[:, 32:40], gsc[:, 32:40], gsc[:, 0:8], ALU.add), ("gsc" + sfx,), ("gsc" + sfx,))
        S.op("vector", lambda e: e.tensor_tensor(gsc[:, 40:48], gsc[:, 32:40], gtm[:, 0:8], ALU.mult), ("gsc" + sfx, "gtm" + sfx), ("gsc" + sfx,))
        S.op("vector", lambda e: e.reduce_sum(sAx[:, 6:7], gsc[:, 40:48], axis=AXX), ("gsc" + sfx,), ("sA" + sfx,))
        S.op("vector", lambda e: e.reciprocal(sAx[:, 7:8], sAx[:, 6:7]), ("sA" + sfx,), ("sA" + sfx,))
        S.op("vector", lambda e: e.tensor_scalar(gtm[:, 0:8], gsc[:, 40:48], sAx[:, 7:8], None, ALU.mult), ("gsc" + sfx, "sA" + sfx), ("gtm" + sfx,))
        if not astep(): return
        S.op("tensor", lambda e: e.transpose(pC[0:8, 128:256], gtm[:, 0:8], eyeT[:]), ("gtm" + sfx, "eyeT"), ("pC",))
        bo_ = (blk % 16) * 128
        S.op("scalar", lambda e, bo_=bo_: e.activation(gstage[0:8, bo_:bo_ + 128], pC[0:8, 128:256], AF.Copy), ("pC",), ("gstage",))
        if not astep(): return
        S.op("tensor", lambda e: e.transpose(pC[0:1, 256:384], sAx[:, 5:6], eyeT[:]), ("sA" + sfx, "eyeT"), ("pC",))
        S.op("scalar", lambda e, bo_=bo_: e.activation(s1stage[0:1, bo_:bo_ + 128], pC[0:1, 256:384], AF.Copy), ("pC",), ("s1stage",))
        if blk % 16 == 15:
            sup = blk // 16
            dma("sync", gates_dram[:, sup * 2048:(sup + 1) * 2048], gstage[0:8, :], ("gstage",), ("gates_dram",))
            dma("sync", s1_dram[:, sup * 2048:(sup + 1) * 2048], s1stage[0:1, :], ("s1stage",), ("s1_dram",))


    for blk in range(_nblkA if "A" in phases else 0):
        _actr[0] = 0
        emit_A_block(blk, AB[blk % 2], str(blk % 2))
    S.barrier()
    if "P" in phases:
        for i, w in enumerate((wqT, wkT, wvT, woT)):
            dma("gpsimd", big1[:, i * 8192:(i + 1) * 8192].rearrange("p (c n) -> p c n", c=8),
                w.rearrange("(c p) n -> p c n", p=128), (), ("big1", "big1s"))

    qwP = (b0(0, 2048), b0(2048, 2048))
    kwP = (b0(4096, 2048), b0(6144, 2048))
    vwP = (b0(8192, 2048), b0(10240, 2048))
    utP = (b0(12288, 2048), b0(14336, 2048))
    h1P = (b0(16384, 2048), b0(18432, 2048))
    s1tP = (ar(11008, TA)[0:1, :], ar(11264, TA)[0:1, :])
    rq = ("qwA", "qwB"); rk = ("kwA", "kwB"); rv = ("vwA", "vwB")
    ru = ("utA", "utB"); rh = ("h1A", "h1B"); rs1 = ("s1tA", "s1tB")

    def emit_P1_tile(tile):
        par = tile % 2
        t0 = tile * TA
        ut, h1, qw, kw, vw, s1t = utP[par], h1P[par], qwP[par], kwP[par], vwP[par], s1tP[par]
        dma("sync", ut.rearrange("p (c t) -> p c t", c=8),
            u_dram[:, t0:t0 + TA].rearrange("(c p) t -> p c t", p=128), ("u_dram",), (ru[par],))
        dma("sync", s1t, s1_dram[:, t0:t0 + TA], ("s1_dram",), (rs1[par],))
        for c in range(8):
            S.op("tensor", lambda e, c=c, s1t=s1t: e.matmul(pBig[:, c * TA:(c + 1) * TA], g1row[:, c * 128:(c + 1) * 128],
                 s1t, start=True, stop=True), ("g1row", rs1[par]), pbk(c * TA, (c + 1) * TA))
        S.op("vector", lambda e, ut=ut, h1=h1: e.tensor_tensor(h1, ut, pBig[:, 0:8 * TA], ALU.mult),
             (ru[par],) + pbk(0, 8 * TA), (rh[par],))
        for c in range(8):
            S.op("vector", lambda e, h1=h1, c=c: e.tensor_scalar(h1[:, c * TA:(c + 1) * TA], h1[:, c * TA:(c + 1) * TA],
                                                                 gbT[:, 16 + c:16 + c + 1], None, ALU.add), (rh[par], "gbT"), (rh[par],))
        for dst, rdst, woff, scl, wdr, rwdr in ((qw, rq[par], 0, SCALE, q_dram, "q_dram"),
                                                (kw, rk[par], 8192, None, k_dram, "k_dram"),
                                                (vw, rv[par], 16384, None, v_dram, "v_dram")):
            for oc in range(8):
                pp, rp = (pA, "pA") if oc % 2 == 0 else (pB, "pB")
                for kc in range(8):
                    S.op("tensor", lambda e, pp=pp, woff=woff, oc=oc, kc=kc, h1=h1:
                         e.matmul(pp[:, 0:TA], big1[:, woff + kc * 1024 + oc * 128:woff + kc * 1024 + oc * 128 + 128],
                                  h1[:, kc * TA:(kc + 1) * TA], start=(kc == 0), stop=(kc == 7)), ("big1", rh[par]), (rp,))
                if scl is None:
                    S.op("scalar", lambda e, dst=dst, oc=oc, pp=pp: e.activation(dst[:, oc * TA:(oc + 1) * TA], pp[:, 0:TA], AF.Copy),
                         (rp,), (rdst,))
                else:
                    S.op("scalar", lambda e, dst=dst, oc=oc, pp=pp, scl=scl:
                         e.activation(dst[:, oc * TA:(oc + 1) * TA], pp[:, 0:TA], AF.Copy, scale=scl), (rp,), (rdst,))
            dma("gpsimd", wdr[:, t0:t0 + TA].rearrange("(c p) t -> p c t", p=128),
                dst.rearrange("p (c t) -> p c t", c=8), (rdst,), (rwdr,))

    for tile in range(NTILE if "P" in phases else 0):
        emit_P1_tile(tile)

    S.barrier()
    u0wP = (b0(16384, 2048), b0(18432, 2048))
    pb0P = (b0(20480, 2048), b0(22528, 2048))
    pb1P = (b0(24576, 2048), b0(26624, 2048))
    csP = (b0(28672, 2048), b0(30720, 2048))
    ru0 = ("u0wA", "u0wB"); rb0 = ("pb0A", "pb0B"); rb1 = ("pb1A", "pb1B"); rcs = ("csA", "csB")
    krepP = ((ar(0, 1024), ar(1024, 1024)), (ar(2048, 1024), ar(3072, 1024)))
    wbcP = (ar(4096, 2048), ar(6144, 2048))
    rstdAP = (ar(8192, 2 * TA, F32), ar(9216, 2 * TA, F32))
    rstdBs = ar(10240, TA)
    vrepsP = (ar(10496, TA), ar(10752, TA))
    scoreR = (hbuf, big1)
    rscR = ("hbuf", "big1s")

    _p2stages = int(os.environ.get('BISECT_P2STAGES', 99))

    def emit_P2_tile(tile):
        par = tile % 2
        t0 = tile * TA
        SR = scoreR[par]
        scv = SR[:, 0:1536].bitcast(F32)
        ztv = SR[:, 2048:3072].bitcast(F32)
        e1b = SR[0:16, 4096:8192]
        qw, kw, vw, ut = qwP[par], kwP[par], vwP[par], utP[par]
        u0w, cs = u0wP[par], csP[par]
        pbufs = (pb0P[par], pb1P[par])
        rpb = (rb0[par], rb1[par])
        rzt = "zt%d" % par
        rscs = "scs%d" % par
        re1 = "e1b%d" % par
        rsA = "rsA%d" % par
        rstdA = rstdAP[par]
        scn = lambda a: "sc%d_%d" % (par, a)
        expsl = lambda a: SR[0:64, 8192 + a * TA:8192 + (a + 1) * TA]
        dma("gpsimd", qw.rearrange("p (c t) -> p c t", c=8),
            q_dram[:, t0:t0 + TA].rearrange("(c p) t -> p c t", p=128), ("q_dram",), (rq[par],))
        dma("gpsimd", kw.rearrange("p (c t) -> p c t", c=8),
            k_dram[:, t0:t0 + TA].rearrange("(c p) t -> p c t", p=128), ("k_dram",), (rk[par],))
        dma("gpsimd", vw.rearrange("p (c t) -> p c t", c=8),
            v_dram[:, t0:t0 + TA].rearrange("(c p) t -> p c t", p=128), ("v_dram",), (rv[par],))
        dma("sync", ut.rearrange("p (c t) -> p c t", c=8),
            u_dram[:, t0:t0 + TA].rearrange("(c p) t -> p c t", p=128), ("u_dram",), (ru[par],))
        if _p2stages < 2: return

        def emit_ksel(bat):
            kr = krepP[par][bat % 2]
            rkr = "kr%d_%d" % (par, bat % 2)
            for jj in range(4):
                ap_ = bat * 4 + jj
                sgn, hp = ap_ // 16, ap_ % 16
                ck, j = sgn * 4 + hp // 4, hp % 4
                S.op("tensor", lambda e, ck=ck, j=j, kw=kw, jj=jj: e.matmul(
                    pBig[:, jj * TA:(jj + 1) * TA], kselT[:, j * 128:(j + 1) * 128],
                    kw[:, ck * TA:(ck + 1) * TA], start=True, stop=True), ("kselT", rk[par]),
                    pbk(jj * TA, (jj + 1) * TA))
            S.op("scalar", lambda e, kr=kr: e.activation(kr, pBig[:, 0:1024], AF.Copy), pbk(0, 1024), (rkr,))
            return kr, rkr

        krs = emit_ksel(0)
        pend = [None]
        for bat in range(8):
            kr, rkr = krs
            for jj in range(4):
                ap_ = bat * 4 + jj
                pw, rpw = pbufs[jj % 2], rpb[jj % 2]
                slot = ap_ % 4
                ps_, rps = ((pA, "pA"), (pB, "pB"), (pA, "pA"), (pB, "pB"))[slot]
                so = (slot // 2) * TA
                S.op("vector", lambda e, qw=qw, jj=jj, pw=pw, kr=kr: e.tensor_tensor(
                    pw.rearrange("p (c t) -> p c t", c=8),
                    qw.rearrange("p (c t) -> p c t", c=8),
                    kr[:, jj * TA:(jj + 1) * TA].unsqueeze(1).broadcast_to([128, 8, TA]), ALU.mult),
                    (rq[par], rkr), (rpw,))
                if pend[0] is not None:
                    pend[0]()
                    pend[0] = None
                for kc in range(8):
                    S.op("tensor", lambda e, kc=kc, so=so, pw=pw, ps_=ps_: e.matmul(
                        ps_[:, so:so + TA], scT[:, kc * 128:(kc + 1) * 128],
                        pw[:, kc * TA:(kc + 1) * TA], start=(kc == 0), stop=(kc == 7)), ("scT", rpw), (rps,))
                if jj == 1 and bat + 1 < 8:
                    krs = emit_ksel(bat + 1)

                def mk_pend(ap_=ap_, so=so, ps_=ps_, rps=rps):
                    def f():
                        S.op("scalar", lambda e: e.activation(expsl(ap_), ps_[0:64, so:so + TA],
                                                              AF.Exp, bias=negc[:]),
                             (rps, "negc"), (scn(ap_),))
                    return f
                pend[0] = mk_pend()
        pend[0]()
        pend[0] = None
        if _p2stages < 3: return
        S.op("vector", lambda e: e.reduce_sum(ztv[64:128, 0:TA].unsqueeze(2),
                                              SR[0:64, 8192:16384].rearrange("p (a t) -> p t a", a=32), axis=AXX),
             tuple(scn(a) for a in range(32)), (rzt,))
        S.op("vector", lambda e: e.reciprocal(scv[32:48, 0:TA], ztv[96:112, 0:TA]), (rzt,), (rscs,))
        S.op("vector", lambda e: e.tensor_copy(scv[32:48, TA:2 * TA], ztv[64:80, 0:TA]), (rzt,), (rscs,))
        S.op("vector", lambda e: e.tensor_tensor(scv[32:48, 2 * TA:3 * TA], scv[32:48, TA:2 * TA],
                                                 scv[32:48, 0:TA], ALU.mult), (rscs,), (rscs,))
        S.op("vector", lambda e: e.tensor_scalar(scv[32:48, 2 * TA:3 * TA], scv[32:48, 2 * TA:3 * TA],
                                                 lcol[32:48, :], None, ALU.mult), (rscs, "lcol"), (rscs,))
        S.op("vector", lambda e, e1b=e1b: e.tensor_tensor(
            e1b.rearrange("p (a t) -> p a t", t=TA),
            SR[32:48, 8192 + 16 * TA:8192 + 32 * TA].rearrange("p (a t) -> p a t", t=TA),
            scv[32:48, 2 * TA:3 * TA].unsqueeze(1).broadcast_to([16, 16, TA]), ALU.mult),
            tuple(scn(a) for a in range(16, 32)) + (rscs,), (re1,))
        wallv = SR[0:16, 0:4096]
        rwall = "wall%d" % par
        S.op("vector", lambda e, wallv=wallv, e1b=e1b: e.tensor_tensor(
            wallv, e1b, SR[0:16, 8192:8192 + 16 * TA], ALU.add),
            (re1,) + tuple(scn(a) for a in range(16)), (rwall,))
        if _p2stages < 4: return
        wb = wbcP[par]
        rwb = "wb%d" % par

        def emit_vcopies(hp):
            cv, ro = hp // 2, (hp % 2) * 64
            vr, rvr = vrepsP[hp % 2], "vr%d" % (hp % 2)
            S.op("gpsimd", lambda e, cv=cv, ro=ro, vw=vw, vr=vr: e.tensor_copy(
                vr[0:64, :], vw[ro:ro + 64, cv * TA:(cv + 1) * TA]), (rv[par],), (rvr,))
            S.op("gpsimd", lambda e, cv=cv, ro=ro, vw=vw, vr=vr: e.tensor_copy(
                vr[64:128, :], vw[ro:ro + 64, cv * TA:(cv + 1) * TA]), (rv[par],), (rvr,))
            return vr, rvr

        vrs = emit_vcopies(0)
        for hp in range(16):
            vr, rvr = vrs
            for qt in range(4):
                po = 1024 + (qt % 2) * 512
                for ci in range(2):
                    c = qt * 2 + ci
                    S.op("tensor", lambda e, c=c, ci=ci, po=po, hp=hp, wallv=wallv: e.matmul(
                        pBig[:, po + ci * TA:po + (ci + 1) * TA], bcT[:, c * 128:(c + 1) * 128],
                        wallv[:, hp * TA:(hp + 1) * TA], start=True, stop=True),
                        ("bcT", rwall), pbk(po + ci * TA, po + (ci + 1) * TA))
                S.op("scalar", lambda e, wb=wb, qt=qt, po=po: e.activation(
                    wb[:, qt * 512:(qt + 1) * 512], pBig[:, po:po + 512], AF.Copy),
                    pbk(po, po + 512), (rwb,))
            if hp < 15:
                vrs = emit_vcopies(hp + 1)
            if hp == 0:
                S.op("vector", lambda e, wb=wb, u0w=u0w, vr=vr: e.tensor_tensor(
                    u0w.rearrange("p (c t) -> p c t", c=8), wb.rearrange("p (c t) -> p c t", c=8),
                    vr.unsqueeze(1).broadcast_to([128, 8, TA]), ALU.mult), (rwb, rvr), (ru0[par],))
            else:
                pv, rpv = pbufs[hp % 2], rpb[hp % 2]
                S.op("vector", lambda e, wb=wb, pv=pv, vr=vr: e.tensor_tensor(
                    pv.rearrange("p (c t) -> p c t", c=8), wb.rearrange("p (c t) -> p c t", c=8),
                    vr.unsqueeze(1).broadcast_to([128, 8, TA]), ALU.mult), (rwb, rvr), (rpv,))
                S.op("vector", lambda e, u0w=u0w, pv=pv: e.tensor_tensor(u0w, u0w, pv, ALU.add),
                     (ru0[par], rpv), (ru0[par],))
        if _p2stages < 5: return
        usq, rusq = pbufs[0], rpb[0]
        S.op("vector", lambda e, usq=usq, u0w=u0w: e.tensor_tensor(usq, u0w, u0w, ALU.mult), (ru0[par],), (rusq,))
        for c in range(8):
            S.op("tensor", lambda e, c=c, usq=usq: e.matmul(pC[:, 0:TA], rrT[:, c * 128:(c + 1) * 128],
                 usq[:, c * TA:(c + 1) * TA], start=(c == 0), stop=(c == 7)), ("rrT", rusq), ("pC",))
        S.op("vector", lambda e: e.tensor_tensor(scv[32:48, 2 * TA:3 * TA], scv[32:48, TA:2 * TA],
                                                 scv[32:48, TA:2 * TA], ALU.mult), (rscs,), (rscs,))
        S.op("vector", lambda e, rstdA=rstdA: e.tensor_scalar(rstdA[0:16, 0:TA], scv[32:48, 2 * TA:3 * TA], EPS, None, ALU.mult),
             (rscs,), (rsA,))
        S.op("vector", lambda e, rstdA=rstdA: e.tensor_tensor(rstdA[0:16, 0:TA], rstdA[0:16, 0:TA], pC[0:16, 0:TA], ALU.add),
             (rsA, "pC"), (rsA,))
        S.op("scalar", lambda e, rstdA=rstdA: e.activation(rstdA[0:16, TA:2 * TA], rstdA[0:16, 0:TA], AF.Ln), (rsA,), (rsA,))
        S.op("scalar", lambda e, rstdA=rstdA: e.activation(rstdBs[0:16, :], rstdA[0:16, TA:2 * TA], AF.Exp, scale=-0.5,
                                              bias=lnl16[:]), (rsA, "lnl16"), ("rstdBs",))
        onw, ronw = pbufs[1], rpb[1]
        for half in range(2):
            for ci in range(4):
                c = half * 4 + ci
                S.op("tensor", lambda e, c=c, ci=ci: e.matmul(pBig[:, 1024 + ci * TA:1024 + (ci + 1) * TA],
                     bcT[:, c * 128:(c + 1) * 128], rstdBs[0:16, :], start=True, stop=True), ("bcT", "rstdBs"),
                     pbk(1024 + ci * TA, 1024 + (ci + 1) * TA))
            S.op("vector", lambda e, onw=onw, u0w=u0w, half=half: e.tensor_tensor(
                onw[:, half * 1024:(half + 1) * 1024], u0w[:, half * 1024:(half + 1) * 1024],
                pBig[:, 1024:2048], ALU.mult), (ru0[par],) + pbk(1024, 2048), (ronw,))
        if _p2stages < 6: return
        for oc in range(8):
            pp, rp = (pC, "pC") if oc % 2 == 0 else (pS, "pS")
            for kc in range(8):
                S.op("tensor", lambda e, pp=pp, oc=oc, kc=kc, onw=onw: e.matmul(
                    pp[:, 0:TA], big1[:, 24576 + kc * 1024 + oc * 128:24576 + kc * 1024 + oc * 128 + 128],
                    onw[:, kc * TA:(kc + 1) * TA], start=(kc == 0), stop=(kc == 7)), ("big1", ronw), (rp,))
            S.op("vector", lambda e, oc=oc, pp=pp, ut=ut, cs=cs, rp=rp: e.scalar_tensor_tensor(
                cs[:, oc * TA:(oc + 1) * TA], pp[:, 0:TA], gbT[:, oc:oc + 1], ut[:, oc * TA:(oc + 1) * TA],
                ALU.add, ALU.add), (rp, "gbT", ru[par]), (rcs[par],))
        dma("sync", conn_dram[:, t0:t0 + TA].rearrange("(c p) t -> p c t", p=128),
            cs.rearrange("p (c t) -> p c t", c=8), (rcs[par],), ("conn_dram",))

    for tile in range(NTILE if "R" in phases else 0):
        emit_P2_tile(tile)

    S.barrier()
    if "Q" in phases:
        dma("gpsimd", big1[:].rearrange("p (c n) -> p c n", c=8),
            w1aT.rearrange("(c p) n -> p c n", p=128), ("big1",), ("big1", "big1s"))
        for dc_ in range(8):
            dma("gpsimd", big0[:, dc_ * 4096:(dc_ + 1) * 4096], w2aT[dc_],
                ("big0_%d" % dc_,), ("big0_%d" % dc_,))

    def mlp_pass(sub, src, rsrc, is_moe, e_idx, skip_src, rskip, post_ffn1=None, post_ffn2=None):
        t0 = sub * TS
        stagA = ar(13312, 512, F32)
        stagB = ar(14336, 512, F32)
        gbc = ar(12288, 512)
        gsl = ar(8192, TS)[0:8, :]
        for fc in range(32):
            slot = fc % 6
            if slot == 0:
                pp, rp = pA[:], "pA"
            elif slot == 1:
                pp, rp = pB[:], "pB"
            else:
                pp, rp = pBig[:, (slot - 2) * TS:(slot - 1) * TS], "pBig%d" % (slot - 2)
            for kc in range(8):
                S.op("tensor", lambda e, pp=pp, fc=fc, kc=kc, src=src: e.matmul(
                    pp, big1[:, kc * 4096 + fc * 128:kc * 4096 + fc * 128 + 128],
                    src[:, kc * TS:(kc + 1) * TS], start=(kc == 0), stop=(kc == 7)), ("big1", rsrc), (rp,))
            if is_moe:
                S.op("scalar", lambda e, fc=fc, pp=pp: e.activation(
                    hbuf[:, fc * TS:(fc + 1) * TS], pp, AF.Gelu, bias=b1mT[:, e_idx * 32 + fc:e_idx * 32 + fc + 1]),
                    (rp, "b1mT"), ("hbuf",))
            else:
                S.op("scalar", lambda e, fc=fc, pp=pp: e.activation(
                    hbuf[:, fc * TS:(fc + 1) * TS], pp, AF.Relu, bias=b1aT[:, fc:fc + 1]), (rp, "b1aT"), ("hbuf",))
        if post_ffn1 is not None:
            post_ffn1()
        if is_moe:
            dma("sync", gsl, gates_dram[:, t0:t0 + TS], ("gates_dram",), ("gsl",))
            S.op("tensor", lambda e: e.matmul(pC[:], eselT[:, e_idx * 128:(e_idx + 1) * 128], gsl,
                                              start=True, stop=True), ("eselT", "gsl"), ("pC",))
            S.op("scalar", lambda e: e.activation(gbc, pC[:], AF.Copy), ("pC",), ("gbc",))
        for dc in range(8):
            if is_moe:
                pp, rp = ((pS, "pS"), (pC, "pC"), (pA, "pA"), (pB, "pB"))[dc % 4]
            else:
                pp, rp = (pS, "pS") if dc % 2 == 0 else (pC, "pC")
            stg, rstg = (stagA, "stagA") if dc % 2 == 0 else (stagB, "stagB")
            for fc in range(32):
                S.op("tensor", lambda e, pp=pp, dc=dc, fc=fc: e.matmul(
                    pp[:], big0[:, dc * 4096 + fc * 128:dc * 4096 + fc * 128 + 128],
                    hbuf[:, fc * TS:(fc + 1) * TS], start=(fc == 0), stop=(fc == 31)), ("big0_%d" % dc, "hbuf"), (rp,))
            if is_moe:
                S.op("vector", lambda e, dc=dc, pp=pp, stg=stg: e.scalar_tensor_tensor(
                    stg, pp[:], b2mT[:, e_idx * 8 + dc:e_idx * 8 + dc + 1], gbc, ALU.add, ALU.mult),
                    (rp, "b2mT", "gbc"), (rstg,))
                dma("gpsimd", acc_dram[dc * 128:(dc + 1) * 128, t0:t0 + TS], stg, (rstg,), ("acc_dram",),
                    accum_op=ALU.add)
            else:
                S.op("vector", lambda e, dc=dc, pp=pp, stg=stg, skip_src=skip_src: e.scalar_tensor_tensor(
                    stg, pp[:], gbT[:, 40 + dc:40 + dc + 1], skip_src[:, dc * TS:(dc + 1) * TS], ALU.add, ALU.add),
                    (rp, "gbT", rskip), (rstg,))
                dma("sync", acc_dram[dc * 128:(dc + 1) * 128, t0:t0 + TS], stg, (rstg,), ("acc_dram",))
            if post_ffn2 is not None:
                post_ffn2(dc)

    for sub in range(NSUB if "Q" in phases else 0):
        t0 = sub * TS
        connA = ar(0, 4096)
        h2w = ar(4096, 4096)
        sqtmp = ar(8192, 4096)
        st2 = ar(8192, 2048, F32)
        st2b = ar(12288, 1024)
        dma("sync", connA.rearrange("p (c t) -> p c t", c=8),
            conn_dram[:, t0:t0 + TS].rearrange("(c p) t -> p c t", p=128), ("conn_dram",), ("connA",))
        S.op("vector", lambda e: e.tensor_tensor(sqtmp, connA, connA, ALU.mult), ("connA",), ("arR2",))
        for c in range(8):
            S.op("tensor", lambda e, c=c: e.matmul(pA[:], ones128[:], sqtmp[:, c * TS:(c + 1) * TS],
                                                   start=(c == 0), stop=(c == 7)), ("ones128", "arR2"), ("pA",))
        for c in range(8):
            S.op("tensor", lambda e, c=c: e.matmul(pS[:], ones128[:], connA[:, c * TS:(c + 1) * TS],
                                                   start=(c == 0), stop=(c == 7)), ("ones128", "connA"), ("pS",))
        S.op("vector", lambda e: e.tensor_scalar(st2[0:1, TS:2 * TS], pA[0:1, 0:TS], 1.0 / DM, None, ALU.mult), ("pA", "arR2"), ("arR2",))
        S.op("vector", lambda e: e.tensor_scalar(st2[0:1, 0:TS], pS[0:1, 0:TS], 1.0 / DM, None, ALU.mult), ("pS", "arR2"), ("arR2",))
        S.op("vector", lambda e: e.tensor_tensor(st2[0:1, 2 * TS:3 * TS], st2[0:1, 0:TS], st2[0:1, 0:TS], ALU.mult), ("arR2",), ("arR2",))
        S.op("vector", lambda e: e.tensor_tensor(st2[0:1, 2 * TS:3 * TS], st2[0:1, TS:2 * TS], st2[0:1, 2 * TS:3 * TS],
                                                 ALU.subtract), ("arR2",), ("arR2",))
        S.op("scalar", lambda e: e.activation(st2[0:1, 3 * TS:4 * TS], st2[0:1, 2 * TS:3 * TS], AF.Ln, bias=epsc[0:1, :]),
             ("arR2", "epsc"), ("arR2",))
        S.op("scalar", lambda e: e.activation(st2b[0:1, TS:2 * TS], st2[0:1, 3 * TS:4 * TS], AF.Exp, scale=-0.5), ("arR2",), ("st2b",))
        S.op("vector", lambda e: e.tensor_copy(st2b[0:1, 0:TS], st2[0:1, 0:TS]), ("arR2",), ("st2b",))
        S.op("tensor", lambda e: e.matmul(pC[:], ones1b[:], st2b[0:1, 0:TS], start=True, stop=True), ("ones1b", "st2b"), ("pC",))
        for c in range(8):
            S.op("tensor", lambda e, c=c: e.matmul(pBig[:, (c % 4) * TS:(c % 4) * TS + TS], g2row[:, c * 128:(c + 1) * 128],
                                                   st2b[0:1, TS:2 * TS], start=True, stop=True), ("g2row", "st2b"),
                 ("pBig%d" % (c % 4),))
            S.op("vector", lambda e, c=c: e.tensor_tensor(h2w[:, c * TS:(c + 1) * TS], connA[:, c * TS:(c + 1) * TS],
                                                          pC[:], ALU.subtract), ("connA", "pC"), ("h2w",))
            S.op("vector", lambda e, c=c: e.tensor_tensor(h2w[:, c * TS:(c + 1) * TS], h2w[:, c * TS:(c + 1) * TS],
                                                          pBig[:, (c % 4) * TS:(c % 4) * TS + TS], ALU.mult),
                 ("h2w", "pBig%d" % (c % 4)), ("h2w",))
            S.op("vector", lambda e, c=c: e.tensor_scalar(h2w[:, c * TS:(c + 1) * TS], h2w[:, c * TS:(c + 1) * TS],
                                                          gbT[:, 32 + c:32 + c + 1], None, ALU.add), ("h2w", "gbT"), ("h2w",))
        mlp_pass(sub, h2w, "h2w", is_moe=False, e_idx=0, skip_src=connA, rskip="connA")

    S.barrier()

    def load_w1(ei):
        for piece in range(4):
            dma("gpsimd", big1[:, piece * 8192:(piece + 1) * 8192].rearrange("p (c n) -> p c n", c=2),
                w1m[ei].rearrange("(c p) n -> p c n", p=128)[:, piece * 2:piece * 2 + 2, :], ("big1",), ("big1",))

    def load_w2(ei, dcs=range(8)):
        for dc in dcs:
            dma("gpsimd", big0[:, dc * 4096:(dc + 1) * 4096], w2m[ei, dc],
                ("big0_%d" % dc,), ("big0_%d" % dc,))

    if "C" in phases:
        load_w1(0)
        load_w2(0)
    for ei in range(NE if "C" in phases else 0):
        for sub in range(NSUB):
            usA = ar(0, 4096)
            usB = ar(4096, 4096)
            us, rus = (usA, "usA") if (sub + ei) % 2 == 0 else (usB, "usB")
            t0 = sub * TS
            dma("sync", us.rearrange("p (c t) -> p c t", c=8),
                u_dram[:, t0:t0 + TS].rearrange("(c p) t -> p c t", p=128), ("u_dram",), (rus,))
            pf1 = pf2 = None
            if ei + 1 < NE and sub == NSUB - 1:
                pf1 = (lambda ei=ei: load_w1(ei + 1))
                pf2 = (lambda dc, ei=ei: load_w2(ei + 1, dcs=[dc]))
            mlp_pass(sub, us, rus, is_moe=True, e_idx=ei, skip_src=None, rskip=None,
                     post_ffn1=pf1, post_ffn2=pf2)

    S.barrier()
    xtm4 = tuple(b0(p * 2048, 1024, F32) for p in range(4))
    accf4 = tuple(b0(8192 + p * 2048, 1024, F32) for p in range(4))
    outt4 = tuple(b0(16384 + p * 2048, 1024, F32) for p in range(4))

    def dptile(par, c):
        if par == 0:
            return pBig[:, c * 128:(c + 1) * 128], pbk(c * 128, (c + 1) * 128)
        if par == 1:
            return pBig[:, 1024 + c * 128:1024 + (c + 1) * 128], pbk(1024 + c * 128, 1024 + (c + 1) * 128)
        if par == 2:
            return (pA[:, (c % 4) * 128:(c % 4 + 1) * 128], ("pA",)) if c < 4 else \
                   (pB[:, (c % 4) * 128:(c % 4 + 1) * 128], ("pB",))
        return (pC[:, (c % 4) * 128:(c % 4 + 1) * 128], ("pC",)) if c < 4 else \
               (pS[:, (c % 4) * 128:(c % 4 + 1) * 128], ("pS",))

    def emit_D_block(blk, par):
        t0 = blk * 128
        xt, xr = xtm4[par], "xtm4_%d" % par
        accf, racc = accf4[par], "accf%d" % par
        outtm, rout = outt4[par], "outtm%d" % par
        dma("sync", xt, x_d[t0:t0 + 128, :], (), (xr,))
        dma("gpsimd", accf.rearrange("p (c t) -> p c t", c=8),
            acc_dram[:, t0:t0 + 128].rearrange("(c p) t -> p c t", p=128), ("acc_dram",), (racc,))
        for c in range(8):
            pt, prs = dptile(par, c)
            S.op("tensor", lambda e, accf=accf, c=c, pt=pt: e.transpose(pt, accf[:, c * 128:(c + 1) * 128], eyeT[:]),
                 (racc, "eyeT"), prs)
        for c in range(8):
            pt, prs = dptile(par, c)
            S.op("vector", lambda e, c=c, xt=xt, outtm=outtm, pt=pt: e.tensor_tensor(
                outtm[:, c * 128:(c + 1) * 128], xt[:, c * 128:(c + 1) * 128], pt, ALU.add),
                 (xr,) + prs, (rout,))
        dma("sync" if par % 2 == 0 else "gpsimd", y_d[t0:t0 + 128, :], outtm, (rout,), ("y",))

    for blk in range(NBLK if "D" in phases else 0):
        emit_D_block(blk, blk % 4)

    globals()["_S"] = S
    with nc.Block() as block:
        S.emit(block, sems)
    es.close()
    return nc


def _consts():
    bf = ml_dtypes.bfloat16
    sc = np.zeros((128, 8 * 128), np.float32)
    for c in range(8):
        s_, j = c // 4, c % 4
        for r in range(128):
            sc[r, c * 128 + s_ * 32 + j * 4 + r // 32] = 1.0
    ks = np.zeros((128, 4 * 128), np.float32)
    for j in range(4):
        for m in range(128):
            ks[j * 32 + (m % 32), j * 128 + m] = 1.0
    bc = np.zeros((16, 8 * 128), np.float32)
    for c in range(8):
        for hl in range(2):
            for d2 in range(64):
                bc[2 * c + hl, c * 128 + hl * 64 + d2] = 1.0
    rr = np.zeros((128, 8 * 128), np.float32)
    for c in range(8):
        for hl in range(2):
            for d2 in range(64):
                rr[hl * 64 + d2, c * 128 + 2 * c + hl] = 1.0 / 64.0
    esel = np.zeros((8, NE * 128), np.float32)
    for e in range(NE):
        esel[e, e * 128:(e + 1) * 128] = 1.0
    return (sc.astype(bf), ks.astype(bf), bc.astype(bf),
            rr.astype(bf), esel.astype(bf), np.eye(128, dtype=np.float32))


def kernel(**inputs):
    bf = ml_dtypes.bfloat16
    x = np.ascontiguousarray(np.asarray(inputs["L0_attn_x"], np.float32).reshape(NTOK, DM))
    Wq = np.asarray(inputs["L0_attn_Wq"], np.float32)
    Wk = np.asarray(inputs["L0_attn_Wk"], np.float32)
    Wv = np.asarray(inputs["L0_attn_Wv"], np.float32)
    Wo = np.asarray(inputs["L0_attn_Wo"], np.float32)
    bo = np.asarray(inputs["L0_attn_bo"], np.float32)
    ln1g = np.asarray(inputs["L0_attn_ln1_g"], np.float32)
    ln1b = np.asarray(inputs["L0_attn_ln1_b"], np.float32)
    ln2g = np.asarray(inputs["L0_attn_ln2_g"], np.float32)
    ln2b = np.asarray(inputs["L0_attn_ln2_b"], np.float32)
    W1a = np.asarray(inputs["L0_attn_W1"], np.float32)
    b1a = np.asarray(inputs["L0_attn_b1"], np.float32)
    W2a = np.asarray(inputs["L0_attn_W2"], np.float32)
    b2a = np.asarray(inputs["L0_attn_b2"], np.float32)
    lq1 = np.asarray(inputs["L0_attn_lq1"], np.float32)
    lq2 = np.asarray(inputs["L0_attn_lq2"], np.float32)
    lk1 = np.asarray(inputs["L0_attn_lk1"], np.float32)
    lk2 = np.asarray(inputs["L0_attn_lk2"], np.float32)
    cent = np.asarray(inputs["L0_ffn_centroids"], np.float32)
    W1m = np.ascontiguousarray(np.asarray(inputs["L0_ffn_W1"], np.float32))
    b1m = np.asarray(inputs["L0_ffn_b1"], np.float32)
    W2m = np.asarray(inputs["L0_ffn_W2"], np.float32)
    W2m = np.ascontiguousarray(W2m.reshape(NE, 32, 128, 8, 128).transpose(0, 3, 2, 1, 4).reshape(NE, 8, 128, 32 * 128))
    b2m = np.asarray(inputs["L0_ffn_b2"], np.float32)

    perm = np.empty(DM, np.int64)
    for s in range(2):
        for h in range(H):
            for d in range(D):
                perm[s * 512 + h * 32 + d] = h * 64 + d * 2 + s
    wqT = np.ascontiguousarray(Wq[perm].T)
    wkT = np.ascontiguousarray(Wk[perm].T)
    wvT = np.ascontiguousarray(Wv.T)
    woT = np.ascontiguousarray(Wo.T)
    w1aT = np.ascontiguousarray(W1a.T)
    w2aT = np.ascontiguousarray(W2a.T.reshape(32, 128, 8, 128).transpose(2, 1, 0, 3).reshape(8, 128, 32 * 128))
    centT = np.ascontiguousarray(cent.T)

    gbrows = np.zeros((8, DM), np.float32)
    gbrows[0] = bo
    gbrows[2] = ln1b
    gbrows[4] = ln2b
    gbrows[5] = b2a
    gbrows = np.ascontiguousarray(gbrows.reshape(8, 8, 128).transpose(2, 0, 1).reshape(128, 64))
    b1a = np.ascontiguousarray(b1a.reshape(32, 128).T)
    b1mT = np.ascontiguousarray(b1m.reshape(NE, 32, 128).transpose(2, 0, 1).reshape(128, NE * 32))
    b2mT = np.ascontiguousarray(b2m.reshape(NE, 8, 128).transpose(2, 0, 1).reshape(128, NE * 8))
    lamv = np.stack([lq1, lq2, lk1, lk2]).astype(np.float32)
    sc, ks, bc, rr, esel, eye = _consts()

    nc = build_program()
    common = dict(wqT=wqT, wkT=wkT, wvT=wvT, woT=woT, w1aT=w1aT, w2aT=w2aT,
                  w1m=W1m, w2m=W2m, centT=centT, gb=gbrows, b1a=b1a,
                  b1m=b1mT, b2m=b2mT, lamv=lamv, cscb=sc, cksel=ks, cbc=bc,
                  crr=rr, cesel=esel, ceye=eye,
                  cg1r=ln1g.reshape(1, DM).astype(bf), cg2r=ln2g.reshape(1, DM).astype(bf))
    in_maps = []
    for c in range(NCORES):
        m = dict(common)
        m["x"] = np.ascontiguousarray(x[c * TPC:(c + 1) * TPC])
        in_maps.append(m)
    res = run_bass_kernel_spmd(nc, in_maps, core_ids=list(range(NCORES)))
    out = np.concatenate([r["y"] for r in res.results], axis=0)
    return out.reshape(32, 2048, DM)


if __name__ == "__main__":
    import importlib.util as _ilu
    spec = _ilu.spec_from_file_location("_r", "/root/problem/reference.py")
    _r = _ilu.module_from_spec(spec)
    spec.loader.exec_module(_r)
    ins = {k: np.asarray(v) for k, v in _r.setup_inputs().items()}
    got = kernel(**ins)
    want = np.asarray(_r.reference(**ins))
    err = np.mean((got - want) ** 2) / np.mean(want ** 2)
    print("residual variance ratio:", err)


# version 20 (final)
# speedup vs baseline: 1.1416x; 1.1416x over previous
import numpy as np
import ml_dtypes
import concourse.bass as bass
import concourse.mybir as mybir
from concourse.bass_utils import run_bass_kernel_spmd
from contextlib import ExitStack

F32 = mybir.dt.float32
BF16 = mybir.dt.bfloat16
AF = mybir.ActivationFunctionType
ALU = mybir.AluOpType
AXX = mybir.AxisListType.X

NCORES = 8
NTOK = 65536
TPC = NTOK // NCORES
DM = 1024
FF = 4096
NE = 8
H = 16
D = 32
SCALE = float(D) ** -0.5
LAMBDA_INIT = 0.8 - 0.6 * float(np.exp(-0.3))
EPS = 1e-5

TA = 256
TS = 512
NBLK = TPC // 128
NTILE = TPC // TA
NSUB = TPC // TS

ALL_Q = ("tensor", "vector", "scalar", "sync", "gpsimd")


class Sched:
    DMAQ = {"sync": "syncD", "gpsimd": "poolD"}

    def __init__(self):
        self.nodes = []
        self.eng_nodes = {e: [] for e in ALL_Q}
        self.last_write = {}
        self.readers = {}
        self.clock = {}
        self.dma_count = {"syncD": 0, "poolD": 0}
        self.cseq = {e: 0 for e in ALL_Q}
        self._bar = None
        self._bar_applied = set()

    EXCLUSIVE = frozenset(("pBig", "pA", "pB", "pC", "pS", "pBig0", "pBig1", "pBig2", "pBig3",
                           "pA0", "pA1", "pB0", "pB1"))

    def op(self, engine, fn, reads=(), writes=(), dma=False, force_sync=False):
        if any(r in self.EXCLUSIVE for r in reads):
            writes = tuple(writes) + tuple(r for r in reads if r in self.EXCLUSIVE)
            reads = tuple(r for r in reads if r not in self.EXCLUSIVE)
        if dma:
            q = self.DMAQ[engine]
            seq = self.dma_count[q]
            self.dma_count[q] += 1
        else:
            q = engine
            seq = self.cseq[engine]
            self.cseq[engine] += 1
        n = dict(engine=engine, fn=fn, dma=dma, queue=q, seq=seq,
                 marked=False, waits=[], clock=None)
        ecl = self.clock.setdefault(engine, {})

        def add_wait(m):
            mq, ms = m["queue"], m["seq"]
            if mq == engine and not m["dma"]:
                if engine == "tensor":
                    return
                if m["seq"] >= seq:
                    return
                if ecl.get(mq, -1) >= ms:
                    return
                m["marked"] = True
                n["waits"].append(m)
                ecl[mq] = ms
                return
            if dma and m["dma"] and m["engine"] == engine:
                return
            if ecl.get(mq, -1) >= ms:
                return
            if not m["dma"]:
                m["marked"] = True
            n["waits"].append(m)
            mc = m["clock"]
            for k, v in mc.items():
                if ecl.get(k, -1) < v:
                    ecl[k] = v
            if ecl.get(mq, -1) < ms:
                ecl[mq] = ms

        if self._bar is not None and engine not in self._bar_applied:
            self._bar_applied.add(engine)
            for m in self._bar:
                add_wait(m)

        for r in reads:
            m = self.last_write.get(r)
            if m is not None:
                add_wait(m)
        for r in writes:
            m = self.last_write.get(r)
            if m is not None:
                add_wait(m)
            for m in self.readers.get(r, {}).values():
                add_wait(m)
        for r in reads:
            self.readers.setdefault(r, {})[q] = n
        for r in writes:
            self.last_write[r] = n
            self.readers[r] = {}
        n["clock"] = dict(ecl)
        n["clock"][q] = seq
        self.nodes.append(n)
        self.eng_nodes[engine].append(n)
        return n

    def barrier(self):
        bar = []
        for e in ALL_Q:
            if self.eng_nodes[e]:
                bar.append(self.eng_nodes[e][-1])
        self._bar = bar
        self._bar_applied = set()

    def emit(self, block, sems):
        for e in ALL_Q:
            cnt = 0
            for n in self.eng_nodes[e]:
                if not n["dma"] and n["marked"]:
                    cnt += 1
                    n["inc_value"] = cnt

        def wait_val(m):
            if m["dma"]:
                return (m["queue"], 16 * (m["seq"] + 1))
            return (m["queue"], m["inc_value"])

        def make_section(ename):
            nodes = self.eng_nodes[ename]

            def section(eng):
                for n in nodes:
                    done = {}
                    for m in n["waits"]:
                        qn, v = wait_val(m)
                        if done.get(qn, -1) >= v:
                            continue
                        done[qn] = v
                        eng.wait_ge(sems[qn], v)
                    inst = n["fn"](eng)
                    if n["dma"]:
                        inst.then_inc(sems[n["queue"]], 16)
                    elif n["marked"]:
                        inst.then_inc(sems[ename], 1)
                if ename == "sync":
                    eng.wait_ge(sems["syncD"], 16 * self.dma_count["syncD"])
                    eng.wait_ge(sems["poolD"], 16 * self.dma_count["poolD"])
            return section

        for e in ALL_Q:
            if self.eng_nodes[e] or e == "sync":
                getattr(block, e)(make_section(e))


PHASES = "APRQCD"


def build_program():
    nc = bass.Bass("TRN2", target_bir_lowering=False, debug=False)

    din = lambda name, shape, dt=F32: nc.dram_tensor(name, list(shape), dt, kind="ExternalInput").ap()
    x_d = din("x", (TPC, DM))
    wqT = din("wqT", (DM, DM))
    wkT = din("wkT", (DM, DM))
    wvT = din("wvT", (DM, DM))
    woT = din("woT", (DM, DM))
    w1aT = din("w1aT", (DM, FF))
    w2aT = din("w2aT", (8, 128, 32 * 128))
    w1m = din("w1m", (NE, DM, FF))
    w2m = din("w2m", (NE, 8, 128, 32 * 128))
    centT_d = din("centT", (DM, NE))
    gb_d = din("gb", (128, 64))
    b1a_d = din("b1a", (128, 32))
    b1m_d = din("b1m", (128, NE * 32))
    b2m_d = din("b2m", (128, NE * 8))
    lam_d = din("lamv", (4, 32))
    csc = nc.dram_tensor("cscb", [128, 8 * 128], BF16, kind="ExternalInput").ap()
    cksel = nc.dram_tensor("cksel", [128, 4 * 128], BF16, kind="ExternalInput").ap()
    cbc = nc.dram_tensor("cbc", [16, 8 * 128], BF16, kind="ExternalInput").ap()
    crr = nc.dram_tensor("crr", [128, 8 * 128], BF16, kind="ExternalInput").ap()
    cesel = nc.dram_tensor("cesel", [8, NE * 128], BF16, kind="ExternalInput").ap()
    ceye = nc.dram_tensor("ceye", [128, 128], F32, kind="ExternalInput").ap()
    cg1r = nc.dram_tensor("cg1r", [1, DM], BF16, kind="ExternalInput").ap()
    cg2r = nc.dram_tensor("cg2r", [1, DM], BF16, kind="ExternalInput").ap()

    y_d = nc.dram_tensor("y", [TPC, DM], F32, kind="ExternalOutput").ap()

    import os as _os2
    _dk = "ExternalOutput" if _os2.environ.get("BISECT_DBGOUT") else "Internal"
    u_dram = nc.dram_tensor("u_dram", [DM, TPC], BF16, kind=_dk).ap()
    conn_dram = nc.dram_tensor("conn_dram", [DM, TPC], BF16, kind=_dk).ap()
    acc_dram = nc.dram_tensor("acc_dram", [DM, TPC], F32, kind=_dk).ap()
    gates_dram = nc.dram_tensor("gates_dram", [NE, TPC], BF16, kind=_dk).ap()
    s1_dram = nc.dram_tensor("s1_dram", [1, TPC], BF16, kind=_dk).ap()
    q_dram = nc.dram_tensor("q_dram", [DM, TPC], BF16, kind="Internal").ap()
    k_dram = nc.dram_tensor("k_dram", [DM, TPC], BF16, kind="Internal").ap()
    v_dram = nc.dram_tensor("v_dram", [DM, TPC], BF16, kind="Internal").ap()

    es = ExitStack()
    sb = lambda n_, s_, d_: es.enter_context(nc.sbuf_tensor(n_, s_, d_))
    psa = lambda n_, s_, d_: es.enter_context(nc.psum_tensor(n_, s_, d_))

    big0 = sb("big0", [128, 32768], BF16)
    import os as _os
    big1 = sb("big1", [128, 512 if _os.environ.get("BISECT_REALBUFS") else 32768], BF16)
    hbuf = sb("hbuf", [128, 16384], BF16)
    aren = sb("aren", [128, 15360], BF16)

    scT = sb("scT", [128, 8 * 128], BF16)
    kselT = sb("kselT", [128, 4 * 128], BF16)
    bcT = sb("bcT", [16, 8 * 128], BF16)
    rrT = sb("rrT", [128, 8 * 128], BF16)
    eselT = sb("eselT", [8, NE * 128], BF16)
    eyeT = sb("eyeT", [128, 128], F32)
    g1row = sb("g1row", [1, DM], BF16)
    g2row = sb("g2row", [1, DM], BF16)
    gbT = sb("gbT", [128, 64], F32)
    b1aT = sb("b1aT", [128, 32], F32)
    b1mT = sb("b1mT", [128, NE * 32], F32)
    b2mT = sb("b2mT", [128, NE * 8], F32)
    centTb = sb("centTb", [128, 8 * 8], F32)
    lq1t = sb("lq1t", [1, 32], F32)
    lq2t = sb("lq2t", [1, 32], F32)
    lk1t = sb("lk1t", [1, 32], F32)
    lk2t = sb("lk2t", [1, 32], F32)
    lsc = sb("lsc", [1, 8], F32)
    lscb = sb("lscb", [1, 2], BF16)
    centHi = sb("centHi", [128, 64], BF16)
    centLo = sb("centLo", [128, 64], BF16)
    ones1b = sb("ones1b", [1, 128], BF16)
    ones128 = sb("ones128", [128, 128], BF16)
    epsc = sb("epsc", [128, 1], F32)
    lcol = sb("lcol", [48, 1], F32)
    lnl16 = sb("lnl16", [16, 1], F32)
    mx16 = sb("mx16", [16, 1], F32)
    negc = sb("negc", [64, 1], F32)

    pBig = psa("pBig", [128, 2048], F32)
    pA = psa("pA", [128, 512], F32)
    pB = psa("pB", [128, 512], F32)
    pC = psa("pC", [128, 512], F32)
    pS = psa("pS", [128, 512], F32)

    sems = {}
    for s in ALL_Q + ("syncD", "poolD"):
        sems[s] = es.enter_context(nc.semaphore("sem_" + s))

    S = Sched()

    def pbk(lo, hi):
        return tuple(f"pBig{b}" for b in range(lo // 512, (hi + 511) // 512))

    def dma(engine, out_ap, in_ap, reads, writes, **kw):
        return S.op(engine, lambda e: e.dma_start(out=out_ap, in_=in_ap, **kw),
                    reads=reads, writes=writes, dma=True)

    def b0(off, n, dt=BF16):
        if dt == BF16:
            return big0[:, off:off + n]
        return big0[:, off:off + 2 * n].bitcast(F32)

    def ar(off, n, dt=BF16):
        if dt == BF16:
            return aren[:, off:off + n]
        return aren[:, off:off + 2 * n].bitcast(F32)

    dma("sync", scT[:], csc, (), ("scT",))
    dma("sync", kselT[:], cksel, (), ("kselT",))
    dma("sync", bcT[:], cbc, (), ("bcT",))
    dma("sync", rrT[:], crr, (), ("rrT",))
    dma("sync", eselT[:], cesel, (), ("eselT",))
    dma("sync", eyeT[:], ceye, (), ("eyeT",))
    dma("sync", g1row[:], cg1r, (), ("g1row",))
    dma("sync", g2row[:], cg2r, (), ("g2row",))
    dma("sync", gbT[:], gb_d, (), ("gbT",))
    dma("sync", b1aT[:], b1a_d, (), ("b1aT",))
    dma("sync", b1mT[:], b1m_d, (), ("b1mT",))
    dma("sync", b2mT[:], b2m_d, (), ("b2mT",))
    dma("sync", centTb[:].rearrange("p (c n) -> p c n", c=8),
        centT_d.rearrange("(c p) n -> p c n", p=128), (), ("centTb",))
    dma("sync", lq1t[:], lam_d[0:1, :], (), ("lq1t",))
    dma("sync", lq2t[:], lam_d[1:2, :], (), ("lq2t",))
    dma("sync", lk1t[:], lam_d[2:3, :], (), ("lk1t",))
    dma("sync", lk2t[:], lam_d[3:4, :], (), ("lk2t",))

    S.op("vector", lambda e: e.memset(ones1b[:], 1.0), (), ("ones1b",))
    S.op("vector", lambda e: e.memset(ones128[:], 1.0), (), ("ones128",))
    S.op("vector", lambda e: e.memset(epsc[:], EPS), (), ("epsc",))
    S.op("vector", lambda e: e.memset(lcol[:], 1.0), (), ("lcol",))
    S.op("vector", lambda e: e.memset(negc[:], -8.0), (), ("negc",))
    S.op("vector", lambda e: e.memset(lnl16[:], float(np.log(1.0 - LAMBDA_INIT))), (), ("lnl16",))

    S.op("vector", lambda e: e.tensor_tensor(lq1t[:], lq1t[:], lk1t[:], ALU.mult), ("lq1t", "lk1t"), ("lq1t",))
    S.op("vector", lambda e: e.tensor_tensor(lq2t[:], lq2t[:], lk2t[:], ALU.mult), ("lq2t", "lk2t"), ("lq2t",))
    S.op("vector", lambda e: e.reduce_sum(lsc[:, 0:1], lq1t[:], axis=AXX), ("lq1t",), ("lsc",))
    S.op("vector", lambda e: e.reduce_sum(lsc[:, 1:2], lq2t[:], axis=AXX), ("lq2t",), ("lsc",))
    S.op("scalar", lambda e: e.activation(lsc[:, 2:3], lsc[:, 0:1], AF.Exp), ("lsc",), ("lsc",))
    S.op("scalar", lambda e: e.activation(lsc[:, 3:4], lsc[:, 1:2], AF.Exp), ("lsc",), ("lsc",))
    S.op("vector", lambda e: e.tensor_tensor(lsc[:, 4:5], lsc[:, 2:3], lsc[:, 3:4], ALU.subtract), ("lsc",), ("lsc",))
    S.op("vector", lambda e: e.tensor_scalar(lsc[:, 4:5], lsc[:, 4:5], float(LAMBDA_INIT), None, ALU.add), ("lsc",), ("lsc",))
    S.op("vector", lambda e: e.tensor_scalar(lsc[:, 5:6], lsc[:, 4:5], -1.0, None, ALU.mult), ("lsc",), ("lsc",))
    S.op("vector", lambda e: e.tensor_copy(lscb[:, 0:1], lsc[:, 5:6]), ("lsc",), ("lscb",))
    S.op("vector", lambda e: e.tensor_tensor(lsc[:, 6:7], lsc[:, 5:6], lscb[:, 0:1], ALU.subtract), ("lsc", "lscb"), ("lsc",))
    S.op("vector", lambda e: e.tensor_copy(lscb[:, 1:2], lsc[:, 6:7]), ("lsc",), ("lscb",))
    S.op("tensor", lambda e: e.matmul(pS[:, 0:1], ones1b[:], lscb[:, 0:1], start=True, stop=False),
         ("ones1b", "lscb"), ("pS",))
    S.op("tensor", lambda e: e.matmul(pS[:, 0:1], ones1b[:], lscb[:, 1:2], start=False, stop=True),
         ("ones1b", "lscb"), ("pS",))
    S.op("vector", lambda e: e.tensor_copy(lcol[32:48, :], pS[32:48, 0:1]), ("pS",), ("lcol",))
    S.op("vector", lambda e: e.tensor_copy(centHi[:], centTb[:]), ("centTb",), ("centHi",))
    S.op("vector", lambda e: e.tensor_tensor(centLo[:], centTb[:], centHi[:], ALU.subtract), ("centTb", "centHi"), ("centLo",))

    import os
    phases = os.environ.get("BISECT_PHASES", PHASES)
    if os.environ.get("BISECT_REALBUFS"):
        xtmA = sb("xtmA_r", [128, 1024], F32)[:]
        xtmB = sb("xtmB_r", [128, 1024], F32)[:]
        xnf = sb("xnf_r", [128, 1024], F32)[:]
        ublk = sb("ublk_r", [128, 1024], BF16)[:]
        bn12 = sb("bn12_r", [128, 12], F32)[:]
        mvv = sb("mv_r", [128, 2], F32)[:]
        sAx = sb("sA_r", [128, 8], F32)[:]
        gtm = sb("gtm_r", [128, 8], F32)[:]
        gsc = sb("gsc_r", [128, 48], F32)[:]
        gstage = sb("gst_r", [128, 2048], BF16)[:]
        s1stage = sb("s1st_r", [128, 2048], BF16)[:]
        g0v = sb("g0_r", [128, 128], F32)[:]
        ulo = sb("ulo_r", [128, 1024], BF16)[:]
        AB = [dict(xnf=xnf, ublk=ublk, ulo=ulo, bn12=bn12, mvv=mvv, sAx=sAx, gtm=gtm, gsc=gsc)] * 2
    else:
        xtmA, xtmB = b0(0, 1024, F32), b0(2048, 1024, F32)
        gstage = b0(7616, 2048, BF16)
        s1stage = b0(9664, 2048, BF16)
        AB = []
        for par in range(2):
            base = 13056 + par * 4096
            AB.append(dict(
                xnf=b0(4096 + par * 1024 * 0, 1024, F32) if False else b0(base, 1024, F32),
                ublk=b0(base + 2048, 1024, BF16),
                ulo=b0(base + 3072, 1024, BF16),
                bn12=b0(11712 + par * 512, 12, F32),
                mvv=b0(11744 + par * 512, 2, F32),
                sAx=b0(11752 + par * 512, 8, F32),
                gtm=b0(11776 + par * 512, 8, F32),
                gsc=b0(11800 + par * 512, 48, F32),
            ))

    glogP = (b0(21504, 128, F32), b0(21760, 128, F32))
    gwkP = tuple(tuple(b0(22016 + p * 1792 + j * 256, 128, F32) for j in range(7)) for p in range(2))
    grP = (b0(25600, 16, F32), b0(25664, 16, F32))
    gr2P = (b0(25632, 16, F32), b0(25696, 16, F32))
    gtmAP = (b0(25728, 128, F32), b0(25984, 128, F32))

    def emit_gate_group(g):
        p = g % 2
        gl, rgl = glogP[p], "glog%d" % p
        A, Bf, C, D, E, F, G = gwkP[p]
        rW = "gwk%d" % p
        r1, r2, rr = grP[p], gr2P[p], "gr%d" % p
        gt, rgt = gtmAP[p], "gtmA%d" % p
        d3 = lambda a: a.rearrange("q (b k) -> q b k", b=16)
        bc = lambda a: a.unsqueeze(2).broadcast_to([128, 16, 8])
        V = lambda f, r, w: S.op("vector", f, r, w)
        V(lambda e: e.tensor_scalar(A, gl, -1.0, None, ALU.mult), (rgl,), (rW,))
        V(lambda e: e.tensor_tensor(Bf, gl, A, ALU.max), (rgl, rW), (rW,))
        V(lambda e: e.tensor_scalar(C, gl, 0.0, None, ALU.is_ge), (rgl,), (rW,))
        S.op("scalar", lambda e: e.activation(D, Bf, AF.Exp, scale=-1.0), (rW,), (rW,))
        V(lambda e: e.tensor_scalar(E, D, 1.0, None, ALU.add), (rW,), (rW,))
        V(lambda e: e.reciprocal(E, E), (rW,), (rW,))
        V(lambda e: e.tensor_tensor(F, D, E, ALU.mult), (rW,), (rW,))
        V(lambda e: e.tensor_tensor(A, E, F, ALU.subtract), (rW,), (rW,))
        V(lambda e: e.tensor_tensor(A, C, A, ALU.mult), (rW,), (rW,))
        V(lambda e: e.tensor_tensor(G, F, A, ALU.add), (rW,), (rW,))
        V(lambda e: e.reduce_max(r1.unsqueeze(2), d3(G), axis=AXX), (rW,), (rr,))
        V(lambda e: e.tensor_tensor(d3(A), d3(G), bc(r1), ALU.is_ge), (rW, rr), (rW,))

        def prefix(src, t1, t2, out):
            V(lambda e: e.tensor_copy(d3(t1)[:, :, 0:1], d3(src)[:, :, 0:1]), (rW,), (rW,))
            V(lambda e: e.tensor_tensor(d3(t1)[:, :, 1:8], d3(src)[:, :, 1:8], d3(src)[:, :, 0:7], ALU.add), (rW,), (rW,))
            V(lambda e: e.tensor_copy(d3(t2)[:, :, 0:2], d3(t1)[:, :, 0:2]), (rW,), (rW,))
            V(lambda e: e.tensor_tensor(d3(t2)[:, :, 2:8], d3(t1)[:, :, 2:8], d3(t1)[:, :, 0:6], ALU.add), (rW,), (rW,))
            V(lambda e: e.tensor_copy(d3(out)[:, :, 0:4], d3(t2)[:, :, 0:4]), (rW,), (rW,))
            V(lambda e: e.tensor_tensor(d3(out)[:, :, 4:8], d3(t2)[:, :, 4:8], d3(t2)[:, :, 0:4], ALU.add), (rW,), (rW,))
            V(lambda e: e.tensor_scalar(out, out, 1.0, None, ALU.is_equal), (rW,), (rW,))

        prefix(A, Bf, C, D)
        V(lambda e: e.tensor_tensor(A, D, A, ALU.mult), (rW,), (rW,))
        V(lambda e: e.scalar_tensor_tensor(Bf, A, -1e30, G, ALU.mult, ALU.add), (rW,), (rW,))
        V(lambda e: e.reduce_max(r2.unsqueeze(2), d3(Bf), axis=AXX), (rW,), (rr,))
        V(lambda e: e.tensor_tensor(d3(C), d3(Bf), bc(r2), ALU.is_ge), (rW, rr), (rW,))
        prefix(C, D, E, F)
        V(lambda e: e.tensor_tensor(C, F, C, ALU.mult), (rW,), (rW,))
        V(lambda e: e.tensor_tensor(A, A, C, ALU.add), (rW,), (rW,))
        V(lambda e: e.tensor_tensor(Bf, A, G, ALU.mult), (rW,), (rW,))
        V(lambda e: e.reduce_sum(r1.unsqueeze(2), d3(Bf), axis=AXX), (rW,), (rr,))
        V(lambda e: e.reciprocal(r1, r1), (rr,), (rr,))
        V(lambda e: e.tensor_tensor(d3(gt), d3(Bf), bc(r1), ALU.mult), (rW, rr), (rgt,))
        for b in range(16):
            ps_, rps_ = (pC, "pC") if b % 2 == 0 else (pB, "pB")
            S.op("tensor", lambda e, b=b, ps_=ps_: e.transpose(ps_[0:8, 128:256], gt[:, b * 8:(b + 1) * 8], eyeT[:]),
                 (rgt, "eyeT"), (rps_,))
            S.op("scalar", lambda e, b=b, ps_=ps_: e.activation(gstage[0:8, b * 128:(b + 1) * 128],
                                                                ps_[0:8, 128:256], AF.Copy), (rps_,), ("gstage",))
        dma("sync", gates_dram[:, g * 2048:(g + 1) * 2048], gstage[0:8, :], ("gstage",), ("gates_dram",))

    _nblkA = int(os.environ.get('BISECT_NBLK', NBLK))
    _aops = int(os.environ.get('BISECT_AOPS', 99))
    _actr = [0]
    def astep():
        _actr[0] += 1
        return _actr[0] <= _aops
    def emit_A_block(blk, P, sfx):
        xnf = P["xnf"]; ublk = P["ublk"]; ulo = P["ulo"]; bn12 = P["bn12"]
        mvv = P["mvv"]; sAx = P["sAx"]; gtm = P["gtm"]; gsc = P["gsc"]
        par = blk % 2
        boff = par * 1024
        psG, rpsG = (pS, "pS") if par == 0 else (pA, "pA")
        xt, xr = (xtmA, "xtmA") if blk % 2 == 0 else (xtmB, "xtmB")
        t0 = blk * 128
        if not astep(): return
        dma("sync", xt, x_d[t0:t0 + 128, :], (), (xr,))
        if not astep(): return
        S.op("vector", lambda e, xt=xt: e.bn_stats(bn12[:, 0:6], xt[:, 0:512]), (xr,), ("bn12" + sfx,))
        S.op("vector", lambda e, xt=xt: e.bn_stats(bn12[:, 6:12], xt[:, 512:1024]), (xr,), ("bn12" + sfx,))
        S.op("vector", lambda e: e.bn_aggr(mvv[:], bn12[:]), ("bn12" + sfx,), ("mv" + sfx,), force_sync=True)
        if not astep(): return
        S.op("scalar", lambda e: e.activation(sAx[:, 0:1], mvv[:, 1:2], AF.Ln, bias=epsc[:]), ("mv" + sfx, "epsc"), ("sA" + sfx,))
        S.op("scalar", lambda e: e.activation(sAx[:, 1:2], sAx[:, 0:1], AF.Exp, scale=-0.5), ("sA" + sfx,), ("sA" + sfx,))
        S.op("scalar", lambda e: e.activation(sAx[:, 2:3], sAx[:, 0:1], AF.Exp, scale=-1.0), ("sA" + sfx,), ("sA" + sfx,))
        S.op("vector", lambda e: e.tensor_tensor(sAx[:, 3:4], mvv[:, 1:2], sAx[:, 2:3], ALU.mult), ("sA" + sfx, "mv" + sfx), ("sA" + sfx,))
        S.op("scalar", lambda e: e.activation(sAx[:, 4:5], sAx[:, 3:4], AF.Ln, bias=epsc[:]), ("sA" + sfx, "epsc"), ("sA" + sfx,))
        S.op("scalar", lambda e: e.activation(sAx[:, 5:6], sAx[:, 4:5], AF.Exp, scale=-0.5), ("sA" + sfx,), ("sA" + sfx,))
        if not astep(): return
        S.op("vector", lambda e, xt=xt: e.tensor_scalar(xnf, xt, mvv[:, 0:1], sAx[:, 1:2], ALU.subtract, ALU.mult),
             (xr, "mv" + sfx, "sA" + sfx), ("xnf" + sfx,))
        if not astep(): return
        for c in range(8):
            S.op("tensor", lambda e, c=c, boff=boff: e.transpose(pBig[:, boff + c * 128:boff + (c + 1) * 128],
                                                                   xnf[:, c * 128:(c + 1) * 128], eyeT[:]),
                 ("xnf" + sfx, "eyeT"), pbk(boff + c * 128, boff + (c + 1) * 128))
        if not astep(): return
        S.op("scalar", lambda e, boff=boff: e.activation(ublk, pBig[:, boff:boff + 1024], AF.Copy), pbk(boff, boff + 1024), ("ublk" + sfx,))
        dma("sync", u_dram[:, t0:t0 + 128].rearrange("(c p) t -> p c t", p=128),
            ublk.rearrange("p (c t) -> p c t", c=8), ("ublk" + sfx,), ("u_dram",))
        if not astep(): return
        S.op("vector", lambda e, boff=boff: e.tensor_copy(xnf, pBig[:, boff:boff + 1024]), pbk(boff, boff + 1024), ("xnf" + sfx,))
        if not astep(): return
        S.op("vector", lambda e: e.tensor_tensor(ulo, xnf, ublk, ALU.subtract), ("xnf" + sfx, "ublk" + sfx), ("ulo" + sfx,))
        if not astep(): return
        for c in range(8):
            for ti, (act, ract, ce, rce) in enumerate(((ublk, "ublk" + sfx, centHi, "centHi"),
                                                       (ublk, "ublk" + sfx, centLo, "centLo"),
                                                       (ulo, "ulo" + sfx, centHi, "centHi"))):
                S.op("tensor", lambda e, c=c, act=act, ce=ce, ti=ti, psG=psG: e.matmul(
                    psG[:, 0:8], act[:, c * 128:(c + 1) * 128], ce[:, c * 8:(c + 1) * 8],
                    start=(c == 0 and ti == 0), stop=(c == 7 and ti == 2)),
                     (ract, rce), (rpsG,))
        if not astep(): return
        gi = blk // 16
        bo_ = (blk % 16) * 128
        S.op("scalar", lambda e, psG=psG, gi=gi: e.activation(
            glogP[gi % 2][:, (blk % 16) * 8:(blk % 16) * 8 + 8], psG[:, 0:8], AF.Copy),
            (rpsG,), ("glog%d" % (gi % 2),))
        if not astep(): return
        S.op("tensor", lambda e: e.transpose(pC[0:1, 256:384], sAx[:, 5:6], eyeT[:]), ("sA" + sfx, "eyeT"), ("pC",))
        S.op("scalar", lambda e, bo_=bo_: e.activation(s1stage[0:1, bo_:bo_ + 128], pC[0:1, 256:384], AF.Copy), ("pC",), ("s1stage",))
        if blk % 16 == 15:
            sup = blk // 16
            dma("sync", s1_dram[:, sup * 2048:(sup + 1) * 2048], s1stage[0:1, :], ("s1stage",), ("s1_dram",))
            emit_gate_group(sup)


    for blk in range(_nblkA if "A" in phases else 0):
        _actr[0] = 0
        emit_A_block(blk, AB[blk % 2], str(blk % 2))
    S.barrier()
    if "P" in phases:
        for i, w in enumerate((wqT, wkT, wvT, woT)):
            dma("gpsimd", big1[:, i * 8192:(i + 1) * 8192].rearrange("p (c n) -> p c n", c=8),
                w.rearrange("(c p) n -> p c n", p=128), (), ("big1", "big1s"))

    qwP = (b0(0, 2048), b0(2048, 2048))
    kwP = (b0(4096, 2048), b0(6144, 2048))
    vwP = (b0(8192, 2048), b0(10240, 2048))
    utP = (b0(12288, 2048), b0(14336, 2048))
    h1P = (b0(16384, 2048), b0(18432, 2048))
    s1tP = (ar(11008, TA)[0:1, :], ar(11264, TA)[0:1, :])
    rq = ("qwA", "qwB"); rk = ("kwA", "kwB"); rv = ("vwA", "vwB")
    ru = ("utA", "utB"); rh = ("h1A", "h1B"); rs1 = ("s1tA", "s1tB")

    def emit_P1_tile(tile):
        par = tile % 2
        t0 = tile * TA
        ut, h1, qw, kw, vw, s1t = utP[par], h1P[par], qwP[par], kwP[par], vwP[par], s1tP[par]
        dma("sync", ut.rearrange("p (c t) -> p c t", c=8),
            u_dram[:, t0:t0 + TA].rearrange("(c p) t -> p c t", p=128), ("u_dram",), (ru[par],))
        dma("sync", s1t, s1_dram[:, t0:t0 + TA], ("s1_dram",), (rs1[par],))
        for c in range(8):
            S.op("tensor", lambda e, c=c, s1t=s1t: e.matmul(pBig[:, c * TA:(c + 1) * TA], g1row[:, c * 128:(c + 1) * 128],
                 s1t, start=True, stop=True), ("g1row", rs1[par]), pbk(c * TA, (c + 1) * TA))
        S.op("vector", lambda e, ut=ut, h1=h1: e.tensor_tensor(h1, ut, pBig[:, 0:8 * TA], ALU.mult),
             (ru[par],) + pbk(0, 8 * TA), (rh[par],))
        for c in range(8):
            S.op("vector", lambda e, h1=h1, c=c: e.tensor_scalar(h1[:, c * TA:(c + 1) * TA], h1[:, c * TA:(c + 1) * TA],
                                                                 gbT[:, 16 + c:16 + c + 1], None, ALU.add), (rh[par], "gbT"), (rh[par],))
        for dst, rdst, woff, scl, wdr, rwdr in ((qw, rq[par], 0, SCALE, q_dram, "q_dram"),
                                                (kw, rk[par], 8192, None, k_dram, "k_dram"),
                                                (vw, rv[par], 16384, None, v_dram, "v_dram")):
            for oc in range(8):
                pp, rp = (pA, "pA") if oc % 2 == 0 else (pB, "pB")
                for kc in range(8):
                    S.op("tensor", lambda e, pp=pp, woff=woff, oc=oc, kc=kc, h1=h1:
                         e.matmul(pp[:, 0:TA], big1[:, woff + kc * 1024 + oc * 128:woff + kc * 1024 + oc * 128 + 128],
                                  h1[:, kc * TA:(kc + 1) * TA], start=(kc == 0), stop=(kc == 7)), ("big1", rh[par]), (rp,))
                if scl is None:
                    S.op("scalar", lambda e, dst=dst, oc=oc, pp=pp: e.activation(dst[:, oc * TA:(oc + 1) * TA], pp[:, 0:TA], AF.Copy),
                         (rp,), (rdst,))
                else:
                    S.op("scalar", lambda e, dst=dst, oc=oc, pp=pp, scl=scl:
                         e.activation(dst[:, oc * TA:(oc + 1) * TA], pp[:, 0:TA], AF.Copy, scale=scl), (rp,), (rdst,))
            dma("gpsimd", wdr[:, t0:t0 + TA].rearrange("(c p) t -> p c t", p=128),
                dst.rearrange("p (c t) -> p c t", c=8), (rdst,), (rwdr,))

    for tile in range(NTILE if "P" in phases else 0):
        emit_P1_tile(tile)

    S.barrier()
    u0wP = (b0(16384, 2048), b0(18432, 2048))
    pb0P = (b0(20480, 2048), b0(22528, 2048))
    pb1P = (b0(24576, 2048), b0(26624, 2048))
    csP = (b0(28672, 2048), b0(30720, 2048))
    ru0 = ("u0wA", "u0wB"); rb0 = ("pb0A", "pb0B"); rb1 = ("pb1A", "pb1B"); rcs = ("csA", "csB")
    krepP = ((ar(0, 1024), ar(1024, 1024)), (ar(2048, 1024), ar(3072, 1024)))
    wbcP = (ar(4096, 2048), ar(6144, 2048))
    rstdAP = (ar(8192, 2 * TA, F32), ar(9216, 2 * TA, F32))
    rstdBs = ar(10240, TA)
    vrepsP = (ar(10496, TA), ar(10752, TA))
    scoreR = (hbuf, big1)
    rscR = ("hbuf", "big1s")

    _p2stages = int(os.environ.get('BISECT_P2STAGES', 99))

    def emit_P2_tile(tile):
        par = tile % 2
        t0 = tile * TA
        SR = scoreR[par]
        scv = SR[:, 0:1536].bitcast(F32)
        ztv = SR[:, 2048:3072].bitcast(F32)
        e1b = SR[0:16, 4096:8192]
        qw, kw, vw, ut = qwP[par], kwP[par], vwP[par], utP[par]
        u0w, cs = u0wP[par], csP[par]
        pbufs = (pb0P[par], pb1P[par])
        rpb = (rb0[par], rb1[par])
        rzt = "zt%d" % par
        rscs = "scs%d" % par
        re1 = "e1b%d" % par
        rsA = "rsA%d" % par
        rstdA = rstdAP[par]
        scn = lambda a: "sc%d_%d" % (par, a)
        expsl = lambda a: SR[0:64, 8192 + a * TA:8192 + (a + 1) * TA]
        dma("gpsimd", qw.rearrange("p (c t) -> p c t", c=8),
            q_dram[:, t0:t0 + TA].rearrange("(c p) t -> p c t", p=128), ("q_dram",), (rq[par],))
        dma("gpsimd", kw.rearrange("p (c t) -> p c t", c=8),
            k_dram[:, t0:t0 + TA].rearrange("(c p) t -> p c t", p=128), ("k_dram",), (rk[par],))
        dma("gpsimd", vw.rearrange("p (c t) -> p c t", c=8),
            v_dram[:, t0:t0 + TA].rearrange("(c p) t -> p c t", p=128), ("v_dram",), (rv[par],))
        dma("sync", ut.rearrange("p (c t) -> p c t", c=8),
            u_dram[:, t0:t0 + TA].rearrange("(c p) t -> p c t", p=128), ("u_dram",), (ru[par],))
        if _p2stages < 2: return

        def emit_ksel(bat):
            kr = krepP[par][bat % 2]
            rkr = "kr%d_%d" % (par, bat % 2)
            for jj in range(4):
                ap_ = bat * 4 + jj
                sgn, hp = ap_ // 16, ap_ % 16
                ck, j = sgn * 4 + hp // 4, hp % 4
                S.op("tensor", lambda e, ck=ck, j=j, kw=kw, jj=jj: e.matmul(
                    pBig[:, jj * TA:(jj + 1) * TA], kselT[:, j * 128:(j + 1) * 128],
                    kw[:, ck * TA:(ck + 1) * TA], start=True, stop=True), ("kselT", rk[par]),
                    pbk(jj * TA, (jj + 1) * TA))
            S.op("scalar", lambda e, kr=kr: e.activation(kr, pBig[:, 0:1024], AF.Copy), pbk(0, 1024), (rkr,))
            return kr, rkr

        krs = emit_ksel(0)
        pend = [None]
        for bat in range(8):
            kr, rkr = krs
            for jj in range(4):
                ap_ = bat * 4 + jj
                pw, rpw = pbufs[jj % 2], rpb[jj % 2]
                slot = ap_ % 4
                ps_, rps = ((pA, "pA"), (pB, "pB"), (pA, "pA"), (pB, "pB"))[slot]
                so = (slot // 2) * TA
                S.op("vector", lambda e, qw=qw, jj=jj, pw=pw, kr=kr: e.tensor_tensor(
                    pw.rearrange("p (c t) -> p c t", c=8),
                    qw.rearrange("p (c t) -> p c t", c=8),
                    kr[:, jj * TA:(jj + 1) * TA].unsqueeze(1).broadcast_to([128, 8, TA]), ALU.mult),
                    (rq[par], rkr), (rpw,))
                if pend[0] is not None:
                    pend[0]()
                    pend[0] = None
                for kc in range(8):
                    S.op("tensor", lambda e, kc=kc, so=so, pw=pw, ps_=ps_: e.matmul(
                        ps_[:, so:so + TA], scT[:, kc * 128:(kc + 1) * 128],
                        pw[:, kc * TA:(kc + 1) * TA], start=(kc == 0), stop=(kc == 7)), ("scT", rpw), (rps,))
                if jj == 1 and bat + 1 < 8:
                    krs = emit_ksel(bat + 1)

                def mk_pend(ap_=ap_, so=so, ps_=ps_, rps=rps):
                    def f():
                        S.op("scalar", lambda e: e.activation(expsl(ap_), ps_[0:64, so:so + TA],
                                                              AF.Exp, bias=negc[:]),
                             (rps, "negc"), (scn(ap_),))
                    return f
                pend[0] = mk_pend()
        pend[0]()
        pend[0] = None
        if _p2stages < 3: return
        S.op("vector", lambda e: e.reduce_sum(ztv[64:128, 0:TA].unsqueeze(2),
                                              SR[0:64, 8192:16384].rearrange("p (a t) -> p t a", a=32), axis=AXX),
             tuple(scn(a) for a in range(32)), (rzt,))
        S.op("vector", lambda e: e.reciprocal(scv[32:48, 0:TA], ztv[96:112, 0:TA]), (rzt,), (rscs,))
        S.op("vector", lambda e: e.tensor_copy(scv[32:48, TA:2 * TA], ztv[64:80, 0:TA]), (rzt,), (rscs,))
        S.op("vector", lambda e: e.tensor_tensor(scv[32:48, 2 * TA:3 * TA], scv[32:48, TA:2 * TA],
                                                 scv[32:48, 0:TA], ALU.mult), (rscs,), (rscs,))
        S.op("vector", lambda e: e.tensor_scalar(scv[32:48, 2 * TA:3 * TA], scv[32:48, 2 * TA:3 * TA],
                                                 lcol[32:48, :], None, ALU.mult), (rscs, "lcol"), (rscs,))
        S.op("vector", lambda e, e1b=e1b: e.tensor_tensor(
            e1b.rearrange("p (a t) -> p a t", t=TA),
            SR[32:48, 8192 + 16 * TA:8192 + 32 * TA].rearrange("p (a t) -> p a t", t=TA),
            scv[32:48, 2 * TA:3 * TA].unsqueeze(1).broadcast_to([16, 16, TA]), ALU.mult),
            tuple(scn(a) for a in range(16, 32)) + (rscs,), (re1,))
        wallv = SR[0:16, 0:4096]
        rwall = "wall%d" % par
        S.op("vector", lambda e, wallv=wallv, e1b=e1b: e.tensor_tensor(
            wallv, e1b, SR[0:16, 8192:8192 + 16 * TA], ALU.add),
            (re1,) + tuple(scn(a) for a in range(16)), (rwall,))
        if _p2stages < 4: return
        wb = wbcP[par]
        rwb = "wb%d" % par

        def emit_vcopies(hp):
            cv, ro = hp // 2, (hp % 2) * 64
            vr, rvr = vrepsP[hp % 2], "vr%d" % (hp % 2)
            S.op("gpsimd", lambda e, cv=cv, ro=ro, vw=vw, vr=vr: e.tensor_copy(
                vr[0:64, :], vw[ro:ro + 64, cv * TA:(cv + 1) * TA]), (rv[par],), (rvr,))
            S.op("gpsimd", lambda e, cv=cv, ro=ro, vw=vw, vr=vr: e.tensor_copy(
                vr[64:128, :], vw[ro:ro + 64, cv * TA:(cv + 1) * TA]), (rv[par],), (rvr,))
            return vr, rvr

        vrs = emit_vcopies(0)
        for hp in range(16):
            vr, rvr = vrs
            for qt in range(4):
                po = 1024 + (qt % 2) * 512
                for ci in range(2):
                    c = qt * 2 + ci
                    S.op("tensor", lambda e, c=c, ci=ci, po=po, hp=hp, wallv=wallv: e.matmul(
                        pBig[:, po + ci * TA:po + (ci + 1) * TA], bcT[:, c * 128:(c + 1) * 128],
                        wallv[:, hp * TA:(hp + 1) * TA], start=True, stop=True),
                        ("bcT", rwall), pbk(po + ci * TA, po + (ci + 1) * TA))
                S.op("scalar", lambda e, wb=wb, qt=qt, po=po: e.activation(
                    wb[:, qt * 512:(qt + 1) * 512], pBig[:, po:po + 512], AF.Copy),
                    pbk(po, po + 512), (rwb,))
            if hp < 15:
                vrs = emit_vcopies(hp + 1)
            if hp == 0:
                S.op("vector", lambda e, wb=wb, u0w=u0w, vr=vr: e.tensor_tensor(
                    u0w.rearrange("p (c t) -> p c t", c=8), wb.rearrange("p (c t) -> p c t", c=8),
                    vr.unsqueeze(1).broadcast_to([128, 8, TA]), ALU.mult), (rwb, rvr), (ru0[par],))
            else:
                pv, rpv = pbufs[hp % 2], rpb[hp % 2]
                S.op("vector", lambda e, wb=wb, pv=pv, vr=vr: e.tensor_tensor(
                    pv.rearrange("p (c t) -> p c t", c=8), wb.rearrange("p (c t) -> p c t", c=8),
                    vr.unsqueeze(1).broadcast_to([128, 8, TA]), ALU.mult), (rwb, rvr), (rpv,))
                S.op("vector", lambda e, u0w=u0w, pv=pv: e.tensor_tensor(u0w, u0w, pv, ALU.add),
                     (ru0[par], rpv), (ru0[par],))
        if _p2stages < 5: return
        usq, rusq = pbufs[0], rpb[0]
        S.op("vector", lambda e, usq=usq, u0w=u0w: e.tensor_tensor(usq, u0w, u0w, ALU.mult), (ru0[par],), (rusq,))
        for c in range(8):
            S.op("tensor", lambda e, c=c, usq=usq: e.matmul(pC[:, 0:TA], rrT[:, c * 128:(c + 1) * 128],
                 usq[:, c * TA:(c + 1) * TA], start=(c == 0), stop=(c == 7)), ("rrT", rusq), ("pC",))
        S.op("vector", lambda e: e.tensor_tensor(scv[32:48, 2 * TA:3 * TA], scv[32:48, TA:2 * TA],
                                                 scv[32:48, TA:2 * TA], ALU.mult), (rscs,), (rscs,))
        S.op("vector", lambda e, rstdA=rstdA: e.tensor_scalar(rstdA[0:16, 0:TA], scv[32:48, 2 * TA:3 * TA], EPS, None, ALU.mult),
             (rscs,), (rsA,))
        S.op("vector", lambda e, rstdA=rstdA: e.tensor_tensor(rstdA[0:16, 0:TA], rstdA[0:16, 0:TA], pC[0:16, 0:TA], ALU.add),
             (rsA, "pC"), (rsA,))
        S.op("scalar", lambda e, rstdA=rstdA: e.activation(rstdA[0:16, TA:2 * TA], rstdA[0:16, 0:TA], AF.Ln), (rsA,), (rsA,))
        S.op("scalar", lambda e, rstdA=rstdA: e.activation(rstdBs[0:16, :], rstdA[0:16, TA:2 * TA], AF.Exp, scale=-0.5,
                                              bias=lnl16[:]), (rsA, "lnl16"), ("rstdBs",))
        onw, ronw = pbufs[1], rpb[1]
        for half in range(2):
            for ci in range(4):
                c = half * 4 + ci
                S.op("tensor", lambda e, c=c, ci=ci: e.matmul(pBig[:, 1024 + ci * TA:1024 + (ci + 1) * TA],
                     bcT[:, c * 128:(c + 1) * 128], rstdBs[0:16, :], start=True, stop=True), ("bcT", "rstdBs"),
                     pbk(1024 + ci * TA, 1024 + (ci + 1) * TA))
            S.op("vector", lambda e, onw=onw, u0w=u0w, half=half: e.tensor_tensor(
                onw[:, half * 1024:(half + 1) * 1024], u0w[:, half * 1024:(half + 1) * 1024],
                pBig[:, 1024:2048], ALU.mult), (ru0[par],) + pbk(1024, 2048), (ronw,))
        if _p2stages < 6: return
        for oc in range(8):
            pp, rp = (pC, "pC") if oc % 2 == 0 else (pS, "pS")
            for kc in range(8):
                S.op("tensor", lambda e, pp=pp, oc=oc, kc=kc, onw=onw: e.matmul(
                    pp[:, 0:TA], big1[:, 24576 + kc * 1024 + oc * 128:24576 + kc * 1024 + oc * 128 + 128],
                    onw[:, kc * TA:(kc + 1) * TA], start=(kc == 0), stop=(kc == 7)), ("big1", ronw), (rp,))
            S.op("vector", lambda e, oc=oc, pp=pp, ut=ut, cs=cs, rp=rp: e.scalar_tensor_tensor(
                cs[:, oc * TA:(oc + 1) * TA], pp[:, 0:TA], gbT[:, oc:oc + 1], ut[:, oc * TA:(oc + 1) * TA],
                ALU.add, ALU.add), (rp, "gbT", ru[par]), (rcs[par],))
        dma("sync", conn_dram[:, t0:t0 + TA].rearrange("(c p) t -> p c t", p=128),
            cs.rearrange("p (c t) -> p c t", c=8), (rcs[par],), ("conn_dram",))

    for tile in range(NTILE if "R" in phases else 0):
        emit_P2_tile(tile)

    S.barrier()
    if "Q" in phases:
        dma("gpsimd", big1[:].rearrange("p (c n) -> p c n", c=8),
            w1aT.rearrange("(c p) n -> p c n", p=128), ("big1",), ("big1", "big1s"))
        for dc_ in range(8):
            dma("gpsimd", big0[:, dc_ * 4096:(dc_ + 1) * 4096], w2aT[dc_],
                ("big0_%d" % dc_,), ("big0_%d" % dc_,))

    def mlp_pass(sub, src, rsrc, is_moe, e_idx, skip_src, rskip, post_ffn1=None, post_ffn2=None):
        t0 = sub * TS
        stagA = ar(13312, 512, F32)
        stagB = ar(14336, 512, F32)
        gbc = ar(12288, 512)
        gsl = ar(8192, TS)[0:8, :]
        for fc in range(32):
            slot = fc % 6
            if slot == 0:
                pp, rp = pA[:], "pA"
            elif slot == 1:
                pp, rp = pB[:], "pB"
            else:
                pp, rp = pBig[:, (slot - 2) * TS:(slot - 1) * TS], "pBig%d" % (slot - 2)
            for kc in range(8):
                S.op("tensor", lambda e, pp=pp, fc=fc, kc=kc, src=src: e.matmul(
                    pp, big1[:, kc * 4096 + fc * 128:kc * 4096 + fc * 128 + 128],
                    src[:, kc * TS:(kc + 1) * TS], start=(kc == 0), stop=(kc == 7)), ("big1", rsrc), (rp,))
            if is_moe:
                S.op("scalar", lambda e, fc=fc, pp=pp: e.activation(
                    hbuf[:, fc * TS:(fc + 1) * TS], pp, AF.Gelu, bias=b1mT[:, e_idx * 32 + fc:e_idx * 32 + fc + 1]),
                    (rp, "b1mT"), ("hbuf",))
            else:
                S.op("scalar", lambda e, fc=fc, pp=pp: e.activation(
                    hbuf[:, fc * TS:(fc + 1) * TS], pp, AF.Relu, bias=b1aT[:, fc:fc + 1]), (rp, "b1aT"), ("hbuf",))
        if post_ffn1 is not None:
            post_ffn1()
        if is_moe:
            dma("sync", gsl, gates_dram[:, t0:t0 + TS], ("gates_dram",), ("gsl",))
            S.op("tensor", lambda e: e.matmul(pC[:], eselT[:, e_idx * 128:(e_idx + 1) * 128], gsl,
                                              start=True, stop=True), ("eselT", "gsl"), ("pC",))
            S.op("scalar", lambda e: e.activation(gbc, pC[:], AF.Copy), ("pC",), ("gbc",))
        for dc in range(8):
            if is_moe:
                pp, rp = ((pS, "pS"), (pC, "pC"), (pA, "pA"), (pB, "pB"))[dc % 4]
            else:
                pp, rp = (pS, "pS") if dc % 2 == 0 else (pC, "pC")
            stg, rstg = (stagA, "stagA") if dc % 2 == 0 else (stagB, "stagB")
            for fc in range(32):
                S.op("tensor", lambda e, pp=pp, dc=dc, fc=fc: e.matmul(
                    pp[:], big0[:, dc * 4096 + fc * 128:dc * 4096 + fc * 128 + 128],
                    hbuf[:, fc * TS:(fc + 1) * TS], start=(fc == 0), stop=(fc == 31)), ("big0_%d" % dc, "hbuf"), (rp,))
            if is_moe:
                S.op("vector", lambda e, dc=dc, pp=pp, stg=stg: e.scalar_tensor_tensor(
                    stg, pp[:], b2mT[:, e_idx * 8 + dc:e_idx * 8 + dc + 1], gbc, ALU.add, ALU.mult),
                    (rp, "b2mT", "gbc"), (rstg,))
                dma("gpsimd", acc_dram[dc * 128:(dc + 1) * 128, t0:t0 + TS], stg, (rstg,), ("acc_dram",),
                    accum_op=ALU.add)
            else:
                S.op("vector", lambda e, dc=dc, pp=pp, stg=stg, skip_src=skip_src: e.scalar_tensor_tensor(
                    stg, pp[:], gbT[:, 40 + dc:40 + dc + 1], skip_src[:, dc * TS:(dc + 1) * TS], ALU.add, ALU.add),
                    (rp, "gbT", rskip), (rstg,))
                dma("sync", acc_dram[dc * 128:(dc + 1) * 128, t0:t0 + TS], stg, (rstg,), ("acc_dram",))
            if post_ffn2 is not None:
                post_ffn2(dc)

    for sub in range(NSUB if "Q" in phases else 0):
        t0 = sub * TS
        connA = ar(0, 4096)
        h2w = ar(4096, 4096)
        sqtmp = ar(8192, 4096)
        st2 = ar(8192, 2048, F32)
        st2b = ar(12288, 1024)
        dma("sync", connA.rearrange("p (c t) -> p c t", c=8),
            conn_dram[:, t0:t0 + TS].rearrange("(c p) t -> p c t", p=128), ("conn_dram",), ("connA",))
        S.op("vector", lambda e: e.tensor_tensor(sqtmp, connA, connA, ALU.mult), ("connA",), ("arR2",))
        for c in range(8):
            S.op("tensor", lambda e, c=c: e.matmul(pA[:], ones128[:], sqtmp[:, c * TS:(c + 1) * TS],
                                                   start=(c == 0), stop=(c == 7)), ("ones128", "arR2"), ("pA",))
        for c in range(8):
            S.op("tensor", lambda e, c=c: e.matmul(pS[:], ones128[:], connA[:, c * TS:(c + 1) * TS],
                                                   start=(c == 0), stop=(c == 7)), ("ones128", "connA"), ("pS",))
        S.op("vector", lambda e: e.tensor_scalar(st2[0:1, TS:2 * TS], pA[0:1, 0:TS], 1.0 / DM, None, ALU.mult), ("pA", "arR2"), ("arR2",))
        S.op("vector", lambda e: e.tensor_scalar(st2[0:1, 0:TS], pS[0:1, 0:TS], 1.0 / DM, None, ALU.mult), ("pS", "arR2"), ("arR2",))
        S.op("vector", lambda e: e.tensor_tensor(st2[0:1, 2 * TS:3 * TS], st2[0:1, 0:TS], st2[0:1, 0:TS], ALU.mult), ("arR2",), ("arR2",))
        S.op("vector", lambda e: e.tensor_tensor(st2[0:1, 2 * TS:3 * TS], st2[0:1, TS:2 * TS], st2[0:1, 2 * TS:3 * TS],
                                                 ALU.subtract), ("arR2",), ("arR2",))
        S.op("scalar", lambda e: e.activation(st2[0:1, 3 * TS:4 * TS], st2[0:1, 2 * TS:3 * TS], AF.Ln, bias=epsc[0:1, :]),
             ("arR2", "epsc"), ("arR2",))
        S.op("scalar", lambda e: e.activation(st2b[0:1, TS:2 * TS], st2[0:1, 3 * TS:4 * TS], AF.Exp, scale=-0.5), ("arR2",), ("st2b",))
        S.op("vector", lambda e: e.tensor_copy(st2b[0:1, 0:TS], st2[0:1, 0:TS]), ("arR2",), ("st2b",))
        S.op("tensor", lambda e: e.matmul(pC[:], ones1b[:], st2b[0:1, 0:TS], start=True, stop=True), ("ones1b", "st2b"), ("pC",))
        for c in range(8):
            S.op("tensor", lambda e, c=c: e.matmul(pBig[:, (c % 4) * TS:(c % 4) * TS + TS], g2row[:, c * 128:(c + 1) * 128],
                                                   st2b[0:1, TS:2 * TS], start=True, stop=True), ("g2row", "st2b"),
                 ("pBig%d" % (c % 4),))
            S.op("vector", lambda e, c=c: e.tensor_tensor(h2w[:, c * TS:(c + 1) * TS], connA[:, c * TS:(c + 1) * TS],
                                                          pC[:], ALU.subtract), ("connA", "pC"), ("h2w",))
            S.op("vector", lambda e, c=c: e.tensor_tensor(h2w[:, c * TS:(c + 1) * TS], h2w[:, c * TS:(c + 1) * TS],
                                                          pBig[:, (c % 4) * TS:(c % 4) * TS + TS], ALU.mult),
                 ("h2w", "pBig%d" % (c % 4)), ("h2w",))
            S.op("vector", lambda e, c=c: e.tensor_scalar(h2w[:, c * TS:(c + 1) * TS], h2w[:, c * TS:(c + 1) * TS],
                                                          gbT[:, 32 + c:32 + c + 1], None, ALU.add), ("h2w", "gbT"), ("h2w",))
        mlp_pass(sub, h2w, "h2w", is_moe=False, e_idx=0, skip_src=connA, rskip="connA")

    S.barrier()

    def load_w1(ei):
        for piece in range(4):
            dma("gpsimd", big1[:, piece * 8192:(piece + 1) * 8192].rearrange("p (c n) -> p c n", c=2),
                w1m[ei].rearrange("(c p) n -> p c n", p=128)[:, piece * 2:piece * 2 + 2, :], ("big1",), ("big1",))

    def load_w2(ei, dcs=range(8)):
        for dc in dcs:
            dma("gpsimd", big0[:, dc * 4096:(dc + 1) * 4096], w2m[ei, dc],
                ("big0_%d" % dc,), ("big0_%d" % dc,))

    if "C" in phases:
        load_w1(0)
        load_w2(0)
    for ei in range(NE if "C" in phases else 0):
        for sub in range(NSUB):
            usA = ar(0, 4096)
            usB = ar(4096, 4096)
            us, rus = (usA, "usA") if (sub + ei) % 2 == 0 else (usB, "usB")
            t0 = sub * TS
            dma("sync", us.rearrange("p (c t) -> p c t", c=8),
                u_dram[:, t0:t0 + TS].rearrange("(c p) t -> p c t", p=128), ("u_dram",), (rus,))
            pf1 = pf2 = None
            if ei + 1 < NE and sub == NSUB - 1:
                pf1 = (lambda ei=ei: load_w1(ei + 1))
                pf2 = (lambda dc, ei=ei: load_w2(ei + 1, dcs=[dc]))
            mlp_pass(sub, us, rus, is_moe=True, e_idx=ei, skip_src=None, rskip=None,
                     post_ffn1=pf1, post_ffn2=pf2)

    S.barrier()
    xtm4 = tuple(b0(p * 2048, 1024, F32) for p in range(4))
    accf4 = tuple(b0(8192 + p * 2048, 1024, F32) for p in range(4))
    outt4 = tuple(b0(16384 + p * 2048, 1024, F32) for p in range(4))

    def dptile(par, c):
        if par == 0:
            return pBig[:, c * 128:(c + 1) * 128], pbk(c * 128, (c + 1) * 128)
        if par == 1:
            return pBig[:, 1024 + c * 128:1024 + (c + 1) * 128], pbk(1024 + c * 128, 1024 + (c + 1) * 128)
        if par == 2:
            return (pA[:, (c % 4) * 128:(c % 4 + 1) * 128], ("pA",)) if c < 4 else \
                   (pB[:, (c % 4) * 128:(c % 4 + 1) * 128], ("pB",))
        return (pC[:, (c % 4) * 128:(c % 4 + 1) * 128], ("pC",)) if c < 4 else \
               (pS[:, (c % 4) * 128:(c % 4 + 1) * 128], ("pS",))

    def emit_D_block(blk, par):
        t0 = blk * 128
        xt, xr = xtm4[par], "xtm4_%d" % par
        accf, racc = accf4[par], "accf%d" % par
        outtm, rout = outt4[par], "outtm%d" % par
        dma("sync", xt, x_d[t0:t0 + 128, :], (), (xr,))
        dma("gpsimd", accf.rearrange("p (c t) -> p c t", c=8),
            acc_dram[:, t0:t0 + 128].rearrange("(c p) t -> p c t", p=128), ("acc_dram",), (racc,))
        for c in range(8):
            pt, prs = dptile(par, c)
            S.op("tensor", lambda e, accf=accf, c=c, pt=pt: e.transpose(pt, accf[:, c * 128:(c + 1) * 128], eyeT[:]),
                 (racc, "eyeT"), prs)
        for c in range(8):
            pt, prs = dptile(par, c)
            S.op("vector", lambda e, c=c, xt=xt, outtm=outtm, pt=pt: e.tensor_tensor(
                outtm[:, c * 128:(c + 1) * 128], xt[:, c * 128:(c + 1) * 128], pt, ALU.add),
                 (xr,) + prs, (rout,))
        dma("sync" if par % 2 == 0 else "gpsimd", y_d[t0:t0 + 128, :], outtm, (rout,), ("y",))

    for blk in range(NBLK if "D" in phases else 0):
        emit_D_block(blk, blk % 4)

    globals()["_S"] = S
    with nc.Block() as block:
        S.emit(block, sems)
    es.close()
    return nc


def _consts():
    bf = ml_dtypes.bfloat16
    sc = np.zeros((128, 8 * 128), np.float32)
    for c in range(8):
        s_, j = c // 4, c % 4
        for r in range(128):
            sc[r, c * 128 + s_ * 32 + j * 4 + r // 32] = 1.0
    ks = np.zeros((128, 4 * 128), np.float32)
    for j in range(4):
        for m in range(128):
            ks[j * 32 + (m % 32), j * 128 + m] = 1.0
    bc = np.zeros((16, 8 * 128), np.float32)
    for c in range(8):
        for hl in range(2):
            for d2 in range(64):
                bc[2 * c + hl, c * 128 + hl * 64 + d2] = 1.0
    rr = np.zeros((128, 8 * 128), np.float32)
    for c in range(8):
        for hl in range(2):
            for d2 in range(64):
                rr[hl * 64 + d2, c * 128 + 2 * c + hl] = 1.0 / 64.0
    esel = np.zeros((8, NE * 128), np.float32)
    for e in range(NE):
        esel[e, e * 128:(e + 1) * 128] = 1.0
    return (sc.astype(bf), ks.astype(bf), bc.astype(bf),
            rr.astype(bf), esel.astype(bf), np.eye(128, dtype=np.float32))


def kernel(**inputs):
    bf = ml_dtypes.bfloat16
    x = np.ascontiguousarray(np.asarray(inputs["L0_attn_x"], np.float32).reshape(NTOK, DM))
    Wq = np.asarray(inputs["L0_attn_Wq"], np.float32)
    Wk = np.asarray(inputs["L0_attn_Wk"], np.float32)
    Wv = np.asarray(inputs["L0_attn_Wv"], np.float32)
    Wo = np.asarray(inputs["L0_attn_Wo"], np.float32)
    bo = np.asarray(inputs["L0_attn_bo"], np.float32)
    ln1g = np.asarray(inputs["L0_attn_ln1_g"], np.float32)
    ln1b = np.asarray(inputs["L0_attn_ln1_b"], np.float32)
    ln2g = np.asarray(inputs["L0_attn_ln2_g"], np.float32)
    ln2b = np.asarray(inputs["L0_attn_ln2_b"], np.float32)
    W1a = np.asarray(inputs["L0_attn_W1"], np.float32)
    b1a = np.asarray(inputs["L0_attn_b1"], np.float32)
    W2a = np.asarray(inputs["L0_attn_W2"], np.float32)
    b2a = np.asarray(inputs["L0_attn_b2"], np.float32)
    lq1 = np.asarray(inputs["L0_attn_lq1"], np.float32)
    lq2 = np.asarray(inputs["L0_attn_lq2"], np.float32)
    lk1 = np.asarray(inputs["L0_attn_lk1"], np.float32)
    lk2 = np.asarray(inputs["L0_attn_lk2"], np.float32)
    cent = np.asarray(inputs["L0_ffn_centroids"], np.float32)
    W1m = np.ascontiguousarray(np.asarray(inputs["L0_ffn_W1"], np.float32))
    b1m = np.asarray(inputs["L0_ffn_b1"], np.float32)
    W2m = np.asarray(inputs["L0_ffn_W2"], np.float32)
    W2m = np.ascontiguousarray(W2m.reshape(NE, 32, 128, 8, 128).transpose(0, 3, 2, 1, 4).reshape(NE, 8, 128, 32 * 128))
    b2m = np.asarray(inputs["L0_ffn_b2"], np.float32)

    perm = np.empty(DM, np.int64)
    for s in range(2):
        for h in range(H):
            for d in range(D):
                perm[s * 512 + h * 32 + d] = h * 64 + d * 2 + s
    wqT = np.ascontiguousarray(Wq[perm].T)
    wkT = np.ascontiguousarray(Wk[perm].T)
    wvT = np.ascontiguousarray(Wv.T)
    woT = np.ascontiguousarray(Wo.T)
    w1aT = np.ascontiguousarray(W1a.T)
    w2aT = np.ascontiguousarray(W2a.T.reshape(32, 128, 8, 128).transpose(2, 1, 0, 3).reshape(8, 128, 32 * 128))
    centT = np.ascontiguousarray(cent.T)

    gbrows = np.zeros((8, DM), np.float32)
    gbrows[0] = bo
    gbrows[2] = ln1b
    gbrows[4] = ln2b
    gbrows[5] = b2a
    gbrows = np.ascontiguousarray(gbrows.reshape(8, 8, 128).transpose(2, 0, 1).reshape(128, 64))
    b1a = np.ascontiguousarray(b1a.reshape(32, 128).T)
    b1mT = np.ascontiguousarray(b1m.reshape(NE, 32, 128).transpose(2, 0, 1).reshape(128, NE * 32))
    b2mT = np.ascontiguousarray(b2m.reshape(NE, 8, 128).transpose(2, 0, 1).reshape(128, NE * 8))
    lamv = np.stack([lq1, lq2, lk1, lk2]).astype(np.float32)
    sc, ks, bc, rr, esel, eye = _consts()

    nc = build_program()
    common = dict(wqT=wqT, wkT=wkT, wvT=wvT, woT=woT, w1aT=w1aT, w2aT=w2aT,
                  w1m=W1m, w2m=W2m, centT=centT, gb=gbrows, b1a=b1a,
                  b1m=b1mT, b2m=b2mT, lamv=lamv, cscb=sc, cksel=ks, cbc=bc,
                  crr=rr, cesel=esel, ceye=eye,
                  cg1r=ln1g.reshape(1, DM).astype(bf), cg2r=ln2g.reshape(1, DM).astype(bf))
    in_maps = []
    for c in range(NCORES):
        m = dict(common)
        m["x"] = np.ascontiguousarray(x[c * TPC:(c + 1) * TPC])
        in_maps.append(m)
    res = run_bass_kernel_spmd(nc, in_maps, core_ids=list(range(NCORES)))
    out = np.concatenate([r["y"] for r in res.results], axis=0)
    return out.reshape(32, 2048, DM)


if __name__ == "__main__":
    import importlib.util as _ilu
    spec = _ilu.spec_from_file_location("_r", "/root/problem/reference.py")
    _r = _ilu.module_from_spec(spec)
    spec.loader.exec_module(_r)
    ins = {k: np.asarray(v) for k, v in _r.setup_inputs().items()}
    got = kernel(**ins)
    want = np.asarray(_r.reference(**ins))
    err = np.mean((got - want) ** 2) / np.mean(want ** 2)
    print("residual variance ratio:", err)
